# version 05
# baseline (speedup 1.0000x reference)
import numpy as np
import concourse.bass as bass
import concourse.mybir as mybir
from concourse.bass_utils import run_bass_kernel_spmd

F32 = mybir.dt.float32
BF16 = mybir.dt.bfloat16
AF = mybir.ActivationFunctionType
ALU = mybir.AluOpType

NC_ = 8
T = 2048
D = 2048
KT = 16
HK = 16
HV = 32
C = 128
NCH = 16
HALF = 1024
XW = HALF + 3
EPS6 = 1e-6
EPS5 = 1e-5

ENGS = ("sync", "tensor", "vector", "scalar", "gpsimd")


NLANE = 16


class Em:

    def __init__(self, nc):
        self.nc = nc
        self.sems = {e: nc.alloc_semaphore("s_" + e) for e in ENGS}
        for q in ("sync", "gpsimd"):
            for l in range(NLANE):
                self.sems[f"{q}.{l}"] = nc.alloc_semaphore(f"d_{q}{l}")
        self.cnt = {e: 0 for e in ENGS}
        self.dcnt = {"sync": 0, "gpsimd": 0}
        self.th = {e: [] for e in ENGS}
        self.wm = {e: {} for e in ENGS}

    def _waits(self, eng, deps):
        need = {}
        for d in deps:
            if d is None:
                continue
            se, c = d
            if c > self.wm[eng].get(se, 0):
                need[se] = max(need.get(se, 0), c)
        for se, c in sorted(need.items()):
            self.wm[eng][se] = c
            s = self.sems[se]
            self.th[eng].append(lambda e, s=s, c=c: e.wait_ge(s, c))

    def op(self, eng, fn, deps=(), inc=1):
        self._waits(eng, deps)
        self.cnt[eng] += inc
        tok = (eng, self.cnt[eng])
        s = self.sems[eng]
        self.th[eng].append(lambda e, fn=fn, s=s, i=inc: fn(e).then_inc(s, i))
        return tok

    def dma(self, q, fn, deps=()):
        i = self.dcnt[q]
        lane = i % NLANE
        sname = f"{q}.{lane}"
        deps = list(deps)
        if i >= NLANE:
            deps.append((sname, 16 * (i // NLANE)))
        self._waits(q, deps)
        self.dcnt[q] = i + 1
        tok = (sname, 16 * (i // NLANE + 1))
        s = self.sems[sname]
        self.th[q].append(lambda e, fn=fn, s=s: fn(e).then_inc(s, 16))
        return tok

    def dma_counts(self):
        out = []
        for q in ("sync", "gpsimd"):
            n = self.dcnt[q]
            for l in range(NLANE):
                cnt = (n - l + NLANE - 1) // NLANE if n > l else 0
                if cnt > 0:
                    out.append((f"{q}.{l}", 16 * cnt))
        return out

    def barrier(self):
        toks = [(e, self.cnt[e]) for e in ENGS if self.cnt[e] > 0]
        toks += self.dma_counts()
        for e in ENGS:
            need = [(se, c) for se, c in toks
                    if se != e and c > self.wm[e].get(se, 0)]
            for se, c in need:
                self.wm[e][se] = c
                s = self.sems[se]
                self.th[e].append(lambda en, s=s, c=c: en.wait_ge(s, c))

    def finalize(self):
        nc, th = self.nc, self.th
        with nc.Block() as block:
            @block.sync
            def _(e):
                for f in th["sync"]:
                    f(e)

            @block.tensor
            def _(e):
                for f in th["tensor"]:
                    f(e)

            @block.vector
            def _(e):
                for f in th["vector"]:
                    f(e)

            @block.scalar
            def _(e):
                for f in th["scalar"]:
                    f(e)

            @block.gpsimd
            def _(e):
                for f in th["gpsimd"]:
                    f(e)


class Bank:
    def __init__(self, ap):
        self.ap = ap
        self.w = None
        self.readers = []

    def wdeps(self):
        d = list(self.readers)
        if self.w is not None:
            d.append(self.w)
        self.readers = []
        return d


def build(phases=("p1", "pa", "pb", "p3", "p4"), dbg=()):
    nc = bass.Bass("TRN2", target_bir_lowering=False, debug=False)
    em = Em(nc)

    def din(name, shape, dt=F32):
        return nc.dram_tensor(name, list(shape), dt, kind="ExternalInput").ap()

    xt = din("xt", (D, 3 + T))
    wqkvz = din("wqkvz", (D, 12288))
    wbat = din("wbat", (D, 64))
    convwt = din("convwt", (64 * 128, 4))
    smallp = din("smallp", (64, 2))
    nwcd = din("nwcd", (128, 1))
    woutt = din("woutt", (4096, D))
    wffnt = din("wffnt", (D, D))
    fbiast = din("fbiast", (128, 16))
    rw1td = din("rw1td", (D, 64))
    rb1cd = din("rb1cd", (64, 1))
    rw2td = din("rw2td", (64, 16))
    rb2cd = din("rb2cd", (16, 1))
    gatesd = din("gatesd", (16, 1))
    asttd = din("asttd", (D, 256))
    bsttd = din("bsttd", (256, D))
    reptd = din("reptd", (16, 256))
    identd = din("identd", (128, 128))
    masksd = din("masksd", (128, 384))
    cmaskd = din("cmaskd", (128, 8))

    outt = nc.dram_tensor("outt", [D, T], F32, kind="ExternalOutput").ap()

    def dint(name, shape, dt=F32):
        return nc.dram_tensor(name, list(shape), dt).ap()

    qsp = dint("qsp", (HK * NCH * 128, 128))
    ksp = dint("ksp", (HK * NCH * 128, 128))
    vsp = dint("vsp", (HV * NCH * 128, 128))
    zsp = dint("zsp", (HV * NCH * 128, 128))
    u0sp = dint("u0sp", (HV * NCH * 128, 128))
    mktsp = dint("mktsp", (HV * NCH * 128, 128))
    atsp = dint("atsp", (HV * NCH * 128, 128))
    k2tsp = dint("k2tsp", (HV * NCH * 128, 128))
    cghsp = dint("cghsp", (HV * 128, T), BF16)
    cglsp = dint("cglsp", (HV * 128, T), BF16)
    attnsp = dint("attnsp", (D, T))
    x2sp = dint("x2sp", (D, T))
    ffnsp = dint("ffnsp", (D, T))
    ccin = dint("ccin", (HV * 2 * 128, 128))
    ccout = dint("ccout", (NC_ * HV * 2 * 128, 128))

    dbg_specs = {
        "qsp": (qsp, (HK * NCH * 128, 128)), "ksp": (ksp, (HK * NCH * 128, 128)),
        "vsp": (vsp, (HV * NCH * 128, 128)), "zsp": (zsp, (HV * NCH * 128, 128)),
        "u0sp": (u0sp, (HV * NCH * 128, 128)), "mktsp": (mktsp, (HV * NCH * 128, 128)),
        "atsp": (atsp, (HV * NCH * 128, 128)), "k2tsp": (k2tsp, (HV * NCH * 128, 128)),
        "attnsp": (attnsp, (D, T)), "x2sp": (x2sp, (D, T)), "ffnsp": (ffnsp, (D, T)),
        "ccin": (ccin, (HV * 2 * 128, 128)), "ccout": (ccout, (NC_ * HV * 2 * 128, 128)),
    }
    dbg_out = {}
    for name in dbg:
        if name in dbg_specs:
            dbg_out[name] = nc.dram_tensor(
                "dbg_" + name, list(dbg_specs[name][1]), F32, kind="ExternalOutput").ap()
    if "vsp2" in dbg:
        dbg_out["vsp2"] = nc.dram_tensor(
            "dbg_vsp2", list(dbg_specs["vsp"][1]), F32, kind="ExternalOutput").ap()
    want_sb_dbg = [n for n in dbg if n in ("babuf", "cumh", "cumt", "btt", "sst")]
    for name in want_sb_dbg:
        shp = {"babuf": (64, T), "cumh": (32, T), "cumt": (128, 512),
               "btt": (128, 512), "sst": (128, HV * 128)}[name]
        dbg_out[name] = nc.dram_tensor("dbg_" + name, list(shp), F32,
                                       kind="ExternalOutput").ap()

    def sbt(name, p, w, dt=F32):
        return nc.alloc_sbuf_tensor(name, [p, w], dt).ap()

    IDC = sbt("IDC", 128, 128)
    MSK = sbt("MSK", 128, 384)
    ONE1 = sbt("ONE1", 1, 128)
    ONEC = sbt("ONEC", 128, 1)
    NEG1 = sbt("NEG1", 128, 1)
    EPSC = sbt("EPSC", 128, 2)
    CMASK = sbt("CMASK", 128, 8)
    NWC = sbt("NWC", 128, 1)
    SMALLP = sbt("SMALLP", 64, 4)
    FB = sbt("FB", 128, 16)
    RB1 = sbt("RB1", 64, 1)
    RB2G = sbt("RB2G", 16, 3)
    RW2 = sbt("RW2", 64, 16)
    RW2H = sbt("RW2H", 64, 16, BF16)
    RW2L = sbt("RW2L", 64, 16, BF16)
    REPT = sbt("REPT", 16, 256)
    ONESF = sbt("ONESF", 128, 128)
    ONES32 = sbt("ONES32", 32, 128)
    CW = sbt("CW", 128, 256)
    ROWS = sbt("ROWS", 1, 4160)
    LAMC = sbt("LAMC", 128, NCH * 32)
    babuf = sbt("babuf", 64, T)
    cumh = sbt("cumh", 32, T)
    cumt = sbt("cumt", 128, NCH * 32)
    btt = sbt("btt", 128, NCH * 32)
    nbtt = sbt("nbtt", 128, NCH * 32)
    MISC = sbt("MISC", 128, 160)
    Schain = sbt("Schain", 128, HV * 128)
    PROD = sbt("PROD", 128, HV * 128)
    WBAH = sbt("WBAH", 128, KT * 64, BF16)
    WBAL = sbt("WBAL", 128, KT * 64, BF16)
    STASH = sbt("STASH", 128, 64 * 4)
    ARN = nc.alloc_sbuf_tensor("ARN", [128, 31744], F32)
    ARNa = ARN.ap()
    ARNb = ARN.ap().bitcast(BF16)

    def arn(off, p, w):
        return ARNa[0:p, off:off + w]

    def arnb(off_f32, w_bf16, p=128):
        return ARNb[0:p, 2 * off_f32:2 * off_f32 + w_bf16]

    psum = [nc.alloc_psum_tensor(f"pbk{i}", [128, 512], F32) for i in range(8)]
    banks = [Bank(p.ap()) for p in psum]

    def ld(dst, src, deps=()):
        return em.dma("sync", lambda e, d=dst, s=src: e.dma_start(out=d, in_=s),
                      deps=deps)

    def st(dst, src, deps=()):
        return em.dma("gpsimd", lambda e, d=dst, s=src: e.dma_start(out=d, in_=s),
                      deps=deps)

    def cast(dst, src, deps=()):
        return em.dma("gpsimd", lambda e, d=dst, s=src: e.dma_start(out=d, in_=s),
                      deps=deps)

    def mmop(out, lhsT, rhs, deps=(), start=True, stop=True):
        return em.op("tensor", lambda e, o=out, l=lhsT, r=rhs, a=start, z=stop:
                     e.matmul(o, l, r, start=a, stop=z), deps=deps)

    def mm3op(out, lh, ll, rh, rl, deps=(), start=True, stop=True):
        em.op("tensor", lambda e: e.matmul(out, lh, rh, start=start, stop=False),
              deps=deps)
        em.op("tensor", lambda e: e.matmul(out, lh, rl, start=False, stop=False))
        return em.op("tensor", lambda e: e.matmul(out, ll, rh, start=False, stop=stop))

    def trop(out, src, deps=(), ident=None):
        idn = ident if ident is not None else IDC
        return em.op("tensor", lambda e, o=out, s=src, i=idn: e.transpose(o, s, i),
                     deps=deps)

    def evac(dst, src, deps=()):
        return em.op("scalar", lambda e, d=dst, s=src: e.copy(d, s), deps=deps)

    def act(out, in_, func, deps=(), bias=None, scale=None, accum=None):
        kw = {}
        if bias is not None:
            kw["bias"] = bias
        if scale is not None:
            kw["scale"] = scale
        if accum is not None:
            kw["accum_out"] = accum
        return em.op("scalar", lambda e, o=out, i=in_, f=func, k=kw:
                     e.activation(o, i, f, **k), deps=deps)

    def vt(out, a, b, opx, deps=()):
        return em.op("vector", lambda e: e.tensor_tensor(out, a, b, op=opx),
                     deps=deps)

    def vs(out, a, col, opx, deps=(), col2=None, op2=None):
        if col2 is not None:
            return em.op("vector", lambda e: e.tensor_scalar(
                out, a, col, col2, op0=opx, op1=op2), deps=deps)
        return em.op("vector", lambda e: e.tensor_scalar(
            out, a, col, None, op0=opx), deps=deps)

    def stt(out, in0, scl, in1, op0, op1, deps=()):
        return em.op("vector", lambda e: e.scalar_tensor_tensor(
            out, in0, scl, in1, op0=op0, op1=op1), deps=deps)

    def vss(out, a, const, opx, deps=()):
        return em.op("vector", lambda e: e.tensor_single_scalar(
            out, a, const, op=opx), deps=deps)

    def vcopy(out, a, deps=()):
        return em.op("vector", lambda e: e.tensor_copy(out, a), deps=deps)

    def vmemset(ap_, c, deps=()):
        return em.op("vector", lambda e: e.memset(ap_, c), deps=deps)

    def vrecip(out, a, deps=()):
        return em.op("vector", lambda e: e.reciprocal(out, a), deps=deps)

    def vrmax(out, a, deps=()):
        return em.op("vector", lambda e: e.reduce_max(out, a, axis=mybir.AxisListType.X),
                     deps=deps)

    t_id = ld(IDC, identd)
    t_msk = ld(MSK, masksd)
    t_cm = ld(CMASK, cmaskd)
    t_nw = ld(NWC, nwcd)
    t_sp = ld(SMALLP[:, 0:2], smallp)
    t_fb = ld(FB, fbiast)
    t_rb1 = ld(RB1, rb1cd)
    t_rb2 = ld(RB2G[:, 0:1], rb2cd)
    t_gt = ld(RB2G[:, 1:2], gatesd)
    t_rw2 = ld(RW2, rw2td)
    t_rept = ld(REPT, reptd)
    cw_toks = [ld(CW[:, ci * 4:(ci + 1) * 4], convwt[ci * 128:(ci + 1) * 128, :])
               for ci in range(64)]
    t_one1 = vmemset(ONE1, 1.0)
    t_onec = vmemset(ONEC, 1.0)
    t_neg1 = vmemset(NEG1, -1.0)
    t_onesf = vmemset(ONESF, 1.0)
    t_ones32 = vmemset(ONES32, 1.0)
    e6 = vmemset(EPSC[:, 0:1], EPS6)
    e5 = vmemset(EPSC[:, 1:2], EPS5)
    tk = act(SMALLP[32:64, 2:3], SMALLP[32:64, 0:1], AF.Exp, deps=[t_sp])
    t_nea = vs(SMALLP[32:64, 2:3], SMALLP[32:64, 2:3], NEG1[32:64, :], ALU.mult,
               deps=[tk, t_neg1])
    t_rbg = vt(RB2G[:, 2:3], RB2G[:, 0:1], RB2G[:, 1:2], ALU.add,
               deps=[t_rb2, t_gt])
    t_rw2h = cast(RW2H, RW2, deps=[t_rw2])
    t_rw2l = vt(RW2L, RW2, RW2H, ALU.subtract, deps=[t_rw2h])

    XN_BF_STRIDE = 1040
    XN_LO_BASE = KT * XN_BF_STRIDE
    XST_O = KT * XN_BF_STRIDE + 32
    RAW_O = XST_O + 2 * 512 + 16
    CNV_O = RAW_O + 2 * XW + 16
    WF_O = CNV_O + 2 * XW + 16
    WH_O = WF_O + 2 * 256 + 16
    SQ_O = WH_O + KT * 256 + 16
    XAP_O = SQ_O + XW + 16

    def conv_ci(m):
        h, sub = m // 6, m % 6
        if sub >= 4:
            return None
        if sub == 0:
            return h
        if sub == 1:
            return 16 + h
        return 32 + 2 * h + (sub - 2)

    def spill_dst(m):
        h, sub = m // 6, m % 6
        if sub == 0:
            return qsp, h
        if sub == 1:
            return ksp, h
        if sub in (2, 3):
            return vsp, 2 * h + (sub - 2)
        return zsp, 2 * h + (sub - 4)

    def phase1():
        xstg = [arn(XST_O + i * 512, 128, 512) for i in range(2)]
        raws = [arn(RAW_O + i * XW, 128, XW) for i in range(2)]
        cnvs = [arn(CNV_O + i * XW, 128, XW) for i in range(2)]
        wf = [arn(WF_O + i * 256, 128, 256) for i in range(2)]
        whi = [arnb(WH_O + k * 256, 256) for k in range(KT)]
        wlo = [arnb(WH_O + k * 256 + 128, 256) for k in range(KT)]
        sqs = arn(SQ_O, 128, XW)
        xap = [arn(XAP_O + i * 512, 128, 512) for i in range(2)]
        xnf = [arn(XAP_O + 1024 + 16 + i * 512, 128, 512) for i in range(2)]
        x_tok = [None, None]
        xa_tok = [None, None]
        xf_tok = [None, None]
        wf_tok = [None, None]
        wh_tok = [None] * KT
        raw_tok = [None, None]
        cnv_tok = [None, None]
        stash_toks = {}
        wba_stg = arn(SQ_O, 128, KT * 64)
        t_wba = None
        for k in range(KT):
            t_wba = ld(wba_stg[:, k * 64:(k + 1) * 64],
                       wbat[k * 128:(k + 1) * 128, :], deps=[])
        t_wbah = cast(WBAH, wba_stg, deps=[t_wba])
        t_wbal = vt(WBAL, wba_stg, WBAH, ALU.subtract, deps=[t_wbah])
        sqs_g = [t_wbal]

        for half in range(2):
            hb = half * HALF
            xoff = 0 if half == 0 else XW
            ncol = XW if half == 0 else HALF
            stripes = [(0, 512), (512, 512), (1024, 3)] if half == 0 else \
                      [(0, 512), (512, 512)]

            def xnh(k, s0, w):
                b0_ = k * XN_BF_STRIDE + s0
                return ARNb[:, b0_:b0_ + w]

            def xnl(k, s0, w):
                b0_ = XN_LO_BASE + k * XN_BF_STRIDE + s0
                return ARNb[:, b0_:b0_ + w]

            rsum = lambda s0, w: ROWS[0:1, s0:s0 + w]
            rsq = lambda s0, w: ROWS[0:1, 1040 + s0:1040 + s0 + w]
            rw1 = lambda s0, w: ROWS[0:1, 2080 + s0:2080 + s0 + w]
            rw2 = lambda s0, w: ROWS[0:1, 3120 + s0:3120 + s0 + w]

            ev0 = ev1 = None
            for s0, w in stripes:
                b0, b1 = banks[0], banks[1]
                d0, d1 = b0.wdeps(), b1.wdeps()
                for k in range(KT):
                    slot = k % 2
                    xl = ld(xstg[slot][:, 0:w],
                            xt[k * 128:(k + 1) * 128, xoff + s0:xoff + s0 + w],
                            deps=[x_tok[slot]])
                    sq = act(sqs[:, 0:w], xstg[slot][:, 0:w], AF.Square,
                             deps=[xl, sqs_g[0]])
                    t1 = mmop(b0.ap[0:1, 0:w], ONEC, xstg[slot][:, 0:w],
                              deps=[xl, t_onec] + (d0 if k == 0 else []),
                              start=(k == 0), stop=(k == KT - 1))
                    t2 = mmop(b1.ap[0:1, 0:w], ONEC, sqs[:, 0:w],
                              deps=[sq] + (d1 if k == 0 else []),
                              start=(k == 0), stop=(k == KT - 1))
                    x_tok[slot] = t2
                    sqs_g[0] = t2
                b0.w, b1.w = t1, t2
                ev0 = evac(rsum(s0, w), b0.ap[0:1, 0:w], deps=[t1])
                ev1 = evac(rsq(s0, w), b1.ap[0:1, 0:w], deps=[t2])
                b0.readers.append(ev0)
                b1.readers.append(ev1)
            tmu = vss(rsum(0, ncol), rsum(0, ncol), 1.0 / D, ALU.mult, deps=[ev0])
            tv1 = vss(rsq(0, ncol), rsq(0, ncol), 1.0 / D, ALU.mult, deps=[ev1])
            tm2 = vt(rw1(0, ncol), rsum(0, ncol), rsum(0, ncol), ALU.mult,
                     deps=[tmu])
            tvr = vt(rsq(0, ncol), rsq(0, ncol), rw1(0, ncol), ALU.subtract,
                     deps=[tv1, tm2])
            tsd = act(rsq(0, ncol), rsq(0, ncol), AF.Sqrt, deps=[tvr, e5],
                      bias=EPSC[0:1, 1:2])
            trr = vrecip(rw2(0, ncol), rsq(0, ncol), deps=[tsd])
            tmr = vt(rw1(0, ncol), rsum(0, ncol), rw2(0, ncol), ALU.mult,
                     deps=[trr, tm2])
            for s0, w in stripes:
                b2, b3 = banks[2], banks[3]
                rb_t = mmop(b2.ap[:, 0:w], ONE1, rw2(s0, w),
                            deps=[trr, t_one1] + b2.wdeps())
                mr_t = mmop(b3.ap[:, 0:w], ONE1, rw1(s0, w),
                            deps=[tmr, t_one1] + b3.wdeps())
                b2.w, b3.w = rb_t, mr_t
                for k in range(KT):
                    slot = k % 2
                    xl = ld(xap[slot][:, 0:w],
                            xt[k * 128:(k + 1) * 128, xoff + s0:xoff + s0 + w],
                            deps=[xa_tok[slot]])
                    f1 = vt(xnf[slot][:, 0:w], xap[slot][:, 0:w], b2.ap[:, 0:w],
                            ALU.mult, deps=[xl, rb_t, xf_tok[slot]])
                    f2 = vt(xnf[slot][:, 0:w], xnf[slot][:, 0:w], b3.ap[:, 0:w],
                            ALU.subtract, deps=[f1, mr_t])
                    b2.readers.append(f1)
                    b3.readers.append(f2)
                    hc = cast(xnh(k, s0, w), xnf[slot][:, 0:w], deps=[f2])
                    lc = vt(xnl(k, s0, w), xnf[slot][:, 0:w], xnh(k, s0, w),
                            ALU.subtract, deps=[hc])
                    xa_tok[slot] = f2
                    xf_tok[slot] = lc
            xn_done = [xf_tok[0], xf_tok[1]]

            for mg in range(48):
                m0 = 2 * mg
                qb = [banks[2 + i] for i in range(6)]
                qdeps = [b.wdeps() for b in qb]
                wl_toks = [None] * KT
                for k in range(KT):
                    slot = k % 2
                    wl = ld(wf[slot], wqkvz[k * 128:(k + 1) * 128, m0 * 128:(m0 + 2) * 128],
                            deps=[wf_tok[slot]])
                    wh_ = cast(whi[k], wf[slot], deps=[wl, wh_tok[k]])
                    wl_ = vt(wlo[k], wf[slot], whi[k], ALU.subtract, deps=[wh_])
                    wf_tok[slot] = wl_
                    wl_toks[k] = wl_
                for mi in range(2):
                    for si, (s0, w) in enumerate(stripes):
                        b = qb[mi * 3 + si]
                        for k in range(KT):
                            dps = [wl_toks[k], xn_done[0], xn_done[1]]
                            if k == 0:
                                dps += qdeps[mi * 3 + si]
                            tmm = mm3op(b.ap[:, 0:w],
                                        whi[k][:, mi * 128:(mi + 1) * 128],
                                        wlo[k][:, mi * 128:(mi + 1) * 128],
                                        xnh(k, s0, w), xnl(k, s0, w),
                                        deps=dps, start=(k == 0), stop=(k == KT - 1))
                        b.w = tmm
                for kk in range(KT):
                    wh_tok[kk] = tmm
                for mi in range(2):
                    m = m0 + mi
                    ci = conv_ci(m)
                    rw_ = raws[mi]
                    cv_ = cnvs[mi]
                    evs = []
                    for si, (s0, w) in enumerate(stripes):
                        b = qb[mi * 3 + si]
                        dst_off = s0 if half == 0 else 3 + s0
                        ev = evac(rw_[:, dst_off:dst_off + w], b.ap[:, 0:w],
                                  deps=[b.w, raw_tok[mi]])
                        b.readers.append(ev)
                        evs.append(ev)
                    if half == 1 and ci is not None:
                        evs.append(vcopy(rw_[:, 0:3], STASH[:, ci * 4:ci * 4 + 3],
                                         deps=evs + [stash_toks.get(ci)]))
                    if ci is not None:
                        c0 = vs(cv_[:, 0:HALF], rw_[:, 0:HALF], CW[:, ci * 4:ci * 4 + 1],
                                ALU.mult, deps=evs + [cnv_tok[mi]] + cw_toks)
                        for i in range(1, 4):
                            c0 = stt(cv_[:, 0:HALF], rw_[:, i:i + HALF],
                                     CW[:, ci * 4 + i:ci * 4 + i + 1], cv_[:, 0:HALF],
                                     ALU.mult, ALU.add, deps=[c0])
                        raw_tok[mi] = c0
                        if half == 0:
                            stash_t = vcopy(STASH[:, ci * 4:ci * 4 + 3],
                                            rw_[:, 1024:1027], deps=evs + [c0])
                            stash_toks[ci] = stash_t
                            raw_tok[mi] = stash_t
                        sl = act(cv_[:, 0:HALF], cv_[:, 0:HALF], AF.Silu, deps=[c0])
                    else:
                        sl = act(cv_[:, 0:HALF], rw_[:, 3:3 + HALF], AF.Silu,
                                 deps=evs + [cnv_tok[mi]])
                        raw_tok[mi] = sl
                    post = sl
                    if m % 6 < 2:
                        sq2 = act(sqs[:, 0:HALF], cv_[:, 0:HALF], AF.Square,
                                  deps=[sl, sqs_g[0]])
                        ssr, rr2 = ROWS[0:1, 0:1024], ROWS[0:1, 1040:2064]
                        b5 = banks[0]
                        toks2 = []
                        for sj in (0, 512):
                            d5 = b5.wdeps()
                            tss = mmop(b5.ap[0:1, 0:512], ONEC, sqs[:, sj:sj + 512],
                                       deps=[sq2, t_onec] + d5)
                            b5.w = tss
                            sqs_g[0] = tss
                            ev5 = evac(ssr[0:1, sj:sj + 512], b5.ap[0:1, 0:512],
                                       deps=[tss])
                            b5.readers.append(ev5)
                            toks2.append(ev5)
                        tsq = act(ssr, ssr, AF.Sqrt, deps=toks2 + [e6],
                                  bias=EPSC[0:1, 0:1])
                        trc = vrecip(rr2, ssr, deps=[tsq])
                        b6 = banks[1]
                        for sj in (0, 512):
                            d6 = b6.wdeps()
                            tbc = mmop(b6.ap[:, 0:512], ONE1, rr2[0:1, sj:sj + 512],
                                       deps=[trc, t_one1] + d6)
                            b6.w = tbc
                            post = vt(cv_[:, sj:sj + 512], cv_[:, sj:sj + 512],
                                      b6.ap[:, 0:512], ALU.mult, deps=[tbc, post])
                            b6.readers.append(post)
                    dsp, idx = spill_dst(m)
                    sts = []
                    for c8 in range(8):
                        cg_ = half * 8 + c8
                        r0 = (idx * NCH + cg_) * 128
                        sts.append(st(dsp[r0:r0 + 128, :],
                                      cv_[:, c8 * 128:(c8 + 1) * 128], deps=[post]))
                    cnv_tok[mi] = sts[-1]

            bb = banks[6]
            for s0, w in ((0, 512), (512, 512)):
                dps0 = bb.wdeps()
                xs = (3 + s0) if half == 0 else s0
                for k in range(KT):
                    tmm = mm3op(bb.ap[0:64, 0:w],
                                WBAH[:, k * 64:(k + 1) * 64], WBAL[:, k * 64:(k + 1) * 64],
                                xnh(k, xs, w), xnl(k, xs, w),
                                deps=[t_wbal, xn_done[0], xn_done[1]] +
                                     (dps0 if k == 0 else []),
                                start=(k == 0), stop=(k == KT - 1))
                bb.w = tmm
                evb = evac(babuf[:, hb + s0:hb + s0 + w], bb.ap[0:64, 0:w],
                           deps=[tmm])
                bb.readers.append(evb)

        tb = act(babuf[0:32, :], babuf[0:32, :], AF.Sigmoid, deps=[evb])
        sp1 = ARNa[32:64, RAW_O:RAW_O + T]
        sp2 = ARNa[32:64, CNV_O:CNV_O + T]
        ty = act(babuf[32:64, :], babuf[32:64, :], AF.Identity, deps=[evb, t_sp,
                 raw_tok[0], raw_tok[1], cnv_tok[0], cnv_tok[1]],
                 bias=SMALLP[32:64, 1:2])
        tr_ = act(sp1, babuf[32:64, :], AF.Relu, deps=[ty])
        tab = act(sp2, babuf[32:64, :], AF.Abs, deps=[ty])
        te_ = act(sp2, sp2, AF.Exp, deps=[tab], scale=-1.0)
        tl_ = act(sp2, sp2, AF.Ln, deps=[te_], bias=ONEC[32:64, :])
        tg1 = vt(babuf[32:64, :], sp1, sp2, ALU.add, deps=[tl_, tr_])
        tg2 = vs(babuf[32:64, :], babuf[32:64, :], SMALLP[32:64, 2:3], ALU.mult,
                 deps=[tg1, t_nea])
        gbuf = ARNa[0:32, XST_O:XST_O + T]
        tgm = cast(gbuf, babuf[32:64, :], deps=[tg2, xf_tok[0], xf_tok[1],
                                                cnv_tok[0], cnv_tok[1]])
        cum_toks = []
        for c in range(NCH):
            tc_ = em.op("vector", lambda e, c=c: e.tensor_tensor_scan(
                cumh[:, c * C:(c + 1) * C], ONES32, gbuf[:, c * C:(c + 1) * C],
                0.0, op0=ALU.mult, op1=ALU.add), deps=[tgm, t_ones32])
            cum_toks.append(tc_)
        trt = []
        for c in range(NCH):
            b7 = banks[7]
            d7 = b7.wdeps()
            t1 = trop(b7.ap[:, 0:32], cumh[0:32, c * C:(c + 1) * C],
                      deps=[cum_toks[c], t_id] + d7, ident=IDC[0:32, 0:32])
            b7.w = t1
            ev = evac(cumt[:, c * 32:(c + 1) * 32], b7.ap[:, 0:32], deps=[t1])
            b7.readers.append(ev)
            d7 = b7.wdeps()
            t2 = trop(b7.ap[:, 0:32], babuf[0:32, c * C:(c + 1) * C],
                      deps=[tb, t_id] + d7, ident=IDC[0:32, 0:32])
            b7.w = t2
            ev2 = evac(btt[:, c * 32:(c + 1) * 32], b7.ap[:, 0:32], deps=[t2])
            b7.readers.append(ev2)
            trt.append(ev2)
        tnb = vs(nbtt, btt, NEG1, ALU.mult, deps=trt)
        for name in ("babuf", "cumh", "cumt", "btt"):
            if name in dbg_out:
                srcs = {"babuf": babuf, "cumh": cumh, "cumt": cumt, "btt": btt}
                st(dbg_out[name], srcs[name], deps=[tg2, tnb] + cum_toks)

    def passA():
        names = ["kbuf", "qbuf", "vbuf", "ktok", "vtb", "Dm", "tel", "teu", "tei",
                 "NpA", "NpB", "TNp", "Wm", "bvt", "blk", "k2t", "u0s", "mks",
                 "mkts", "atile", "pct", "tmp32", "psT"]
        SLP = []
        LASTP = []
        for pp in range(2):
            SLP.append({n: arn((pp * len(names) + i) * 132, 128, 128)
                        for i, n in enumerate(names)})
            LASTP.append({n: None for n in names})
        SL = SLP[0]
        last = LASTP[0]
        ch_tok = {}
        prod_init = []
        sch_init = vmemset(Schain, 0.0)
        for vh in range(HV):
            prod_init.append(vcopy(PROD[:, vh * 128:(vh + 1) * 128], IDC, deps=[t_id]))
        schain_tok = [sch_init] * HV
        prod_tok = list(prod_init)

        bG, bW4 = banks[0], banks[0]
        bG2 = banks[4]
        CHAINB = [(banks[1], banks[2], banks[3]), (banks[5], banks[6], banks[7])]
        bW1, bW2, bW3 = CHAINB[0]

        ch_readers = []
        for c in range(NCH):
            SL = SLP[0]
            last = LASTP[0]
            csl = slice(c * 32, (c + 1) * 32)
            t_expc = act(MISC[:, 0:32], cumt[:, csl], AF.Exp,
                         deps=list(ch_readers))
            tm32 = vs(SL["tmp32"][0:32, 0:128], ONESF[0:32, :],
                      cumh[:, c * C + 127:c * C + 128], ALU.mult,
                      deps=[t_onesf, last["tmp32"]])
            d4 = bW4.wdeps()
            tlog = trop(bW4.ap[:, 0:32], SL["tmp32"][0:32, 0:128],
                        deps=[tm32, t_id] + d4, ident=IDC[0:32, 0:32])
            bW4.w = tlog
            te2l = vt(MISC[:, 32:64], bW4.ap[:, 0:32], cumt[:, csl], ALU.subtract,
                      deps=[tlog] + ch_readers)
            te2 = act(MISC[:, 32:64], MISC[:, 32:64], AF.Exp, deps=[te2l])
            tlc = act(MISC[:, 64:96], bW4.ap[:, 0:32], AF.Exp, deps=[tlog, te2l])
            tlcs = vcopy(LAMC[:, csl], MISC[:, 64:96], deps=[tlc])
            bW4.readers += [te2l, tlc]
            last["tmp32"] = te2l
            ch = {"expc": t_expc, "e2m": te2, "lamC": tlc}
            ch_readers = [tlcs]

            for h in range(HK):
                ph = h & 1
                SL = SLP[ph]
                last = LASTP[ph]
                bW1, bW2, bW3 = CHAINB[ph]
                r0 = (h * NCH + c) * 128
                lk = ld(SL["kbuf"], ksp[r0:r0 + 128, :], deps=[last["kbuf"]])
                lq = ld(SL["qbuf"], qsp[r0:r0 + 128, :], deps=[last["qbuf"]])
                dG = bG.wdeps()
                tg1_ = mmop(bG.ap[:, 0:128], SL["kbuf"], SL["kbuf"], deps=[lk] + dG)
                bG.w = tg1_
                dG2 = bG2.wdeps()
                tg2_ = mmop(bG2.ap[:, 0:128], SL["kbuf"], SL["qbuf"], deps=[lk, lq] + dG2)
                bG2.w = tg2_
                dW = bW1.wdeps()
                tkt = trop(bW1.ap[:, 0:128], SL["kbuf"], deps=[lk, t_id] + dW)
                bW1.w = tkt
                ektok = evac(SL["ktok"], bW1.ap[:, 0:128], deps=[tkt, last["ktok"]])
                bW1.readers.append(ektok)
                last["kbuf"], last["qbuf"] = tg2_, tg2_
                last["ktok"] = ektok

                for vh in (2 * h, 2 * h + 1):
                    vr0 = (vh * NCH + c) * 128
                    ccol = cumt[:, c * 32 + vh:c * 32 + vh + 1]
                    bcol = btt[:, c * 32 + vh:c * 32 + vh + 1]
                    nbcol = nbtt[:, c * 32 + vh:c * 32 + vh + 1]
                    lcol = MISC[:, 0 + vh:1 + vh]
                    e2col = MISC[:, 32 + vh:33 + vh]
                    lCcol = MISC[:, 64 + vh:65 + vh]
                    lv = ld(SL["vbuf"], vsp[vr0:vr0 + 128, :], deps=[last["vbuf"]])
                    dW = bW2.wdeps()
                    tvt = trop(bW2.ap[:, 0:128], SL["vbuf"], deps=[lv, t_id] + dW)
                    bW2.w = tvt
                    evt = evac(SL["vtb"], bW2.ap[:, 0:128], deps=[tvt, last["vtb"]])
                    bW2.readers.append(evt)
                    last["vbuf"] = tvt
                    tcc = vs(SL["Dm"], ONESF, ccol, ALU.mult,
                             deps=[t_onesf, last["Dm"]])
                    dW = bW3.wdeps()
                    tcb = trop(bW3.ap[:, 0:128], SL["Dm"], deps=[tcc, t_id] + dW)
                    bW3.w = tcb
                    tdm = vt(SL["Dm"], SL["Dm"], bW3.ap[:, 0:128], ALU.subtract,
                             deps=[tcb])
                    bW3.readers.append(tdm)
                    t_el = vt(SL["tel"], SL["Dm"], MSK[:, 0:128], ALU.add,
                              deps=[tdm, t_msk, last["tel"]])
                    t_el = act(SL["tel"], SL["tel"], AF.Exp, deps=[t_el])
                    t_eu = stt(SL["teu"], SL["Dm"], NEG1[:, 0:1], MSK[:, 128:256],
                               ALU.mult, ALU.add, deps=[tdm, last["teu"], t_neg1])
                    t_eu = act(SL["teu"], SL["teu"], AF.Exp, deps=[t_eu])
                    t_ei = stt(SL["tei"], SL["Dm"], NEG1[:, 0:1], MSK[:, 256:384],
                               ALU.mult, ALU.add, deps=[tdm, last["tei"]])
                    t_ei = act(SL["tei"], SL["tei"], AF.Exp, deps=[t_ei])
                    last["Dm"] = t_ei
                    tnp = vt(SL["NpA"], SL["tel"], bG.ap[:, 0:128], ALU.mult,
                             deps=[t_el, bG.w, last["NpA"]])
                    tnp = vs(SL["NpA"], SL["NpA"], nbcol, ALU.mult,
                             deps=[tnp])
                    last["tel"] = tnp
                    tnbb = vs(SL["tmp32"], ONESF, nbcol, ALU.mult,
                              deps=[t_onesf, last["tmp32"]])
                    dW = bW3.wdeps()
                    tnbt = trop(bW3.ap[:, 0:128], SL["tmp32"], deps=[tnbb, t_id] + dW)
                    bW3.w = tnbt
                    last["tmp32"] = tnbb
                    tb_ = vt(SL["teu"], SL["teu"], bG.ap[:, 0:128], ALU.mult,
                             deps=[t_eu, bG.w])
                    tb_ = vt(SL["teu"], SL["teu"], bW3.ap[:, 0:128], ALU.mult,
                             deps=[tb_, tnbt])
                    bW3.readers.append(tb_)
                    bG.readers += [tnp, tb_]
                    tat = vt(SL["atile"], SL["tei"], bG2.ap[:, 0:128], ALU.mult,
                             deps=[t_ei, bG2.w, last["atile"]])
                    bG2.readers.append(tat)
                    last["tei"] = tat
                    sat = st(atsp[vr0:vr0 + 128, :], SL["atile"], deps=[tat])
                    last["atile"] = sat
                    twm = vt(SL["Wm"], SL["teu"], IDC, ALU.add,
                             deps=[tb_, t_id, last["Wm"]])
                    last["teu"] = twm
                    np_cur = "NpA"
                    np_tok = tnp
                    for j in range(1, 7):
                        dW = bW1.wdeps()
                        t_tr = trop(bW1.ap[:, 0:128], SL[np_cur], deps=[np_tok, t_id] + dW)
                        bW1.w = t_tr
                        e_tr = evac(SL["TNp"], bW1.ap[:, 0:128],
                                    deps=[t_tr, last["TNp"]])
                        bW1.readers.append(e_tr)
                        np_nxt = "NpB" if np_cur == "NpA" else "NpA"
                        dW = bW2.wdeps()
                        t_sq = mmop(bW2.ap[:, 0:128], SL["TNp"], SL[np_cur],
                                    deps=[e_tr, np_tok] + dW)
                        bW2.w = t_sq
                        e_sq = evac(SL[np_nxt], bW2.ap[:, 0:128],
                                    deps=[t_sq, last[np_nxt]])
                        bW2.readers.append(e_sq)
                        last["TNp"] = t_sq
                        dW = bW3.wdeps()
                        t_ch = mmop(bW3.ap[:, 0:128], SL[np_nxt], SL["Wm"],
                                    deps=[e_sq, twm] + dW)
                        bW3.w = t_ch
                        twm = vt(SL["Wm"], SL["Wm"], bW3.ap[:, 0:128], ALU.add,
                                 deps=[t_ch])
                        bW3.readers.append(twm)
                        last[np_cur] = t_sq
                        last[np_nxt] = t_ch
                        np_cur, np_tok = np_nxt, e_sq
                    last["Wm"] = twm
                    tbv = vs(SL["bvt"], SL["vtb"], bcol, ALU.mult,
                             deps=[evt, last["bvt"]])
                    dW = bW1.wdeps()
                    tu0 = mmop(bW1.ap[:, 0:128], SL["Wm"], SL["bvt"],
                               deps=[twm, tbv] + dW)
                    bW1.w = tu0
                    eu0 = evac(SL["u0s"], bW1.ap[:, 0:128], deps=[tu0, last["u0s"]])
                    bW1.readers.append(eu0)
                    su0 = st(u0sp[vr0:vr0 + 128, :], SL["u0s"], deps=[eu0])
                    last["bvt"] = tu0
                    tbl = vt(MISC[:, 128 + ph:129 + ph], bcol, lcol, ALU.mult,
                             deps=[ch["expc"], last.get("blcol")])
                    last["blcol"] = tbl
                    tblk = vs(SL["blk"], SL["ktok"], MISC[:, 128 + ph:129 + ph],
                              ALU.mult, deps=[ektok, tbl, last["blk"]])
                    last["blcol"] = tblk
                    dW = bW2.wdeps()
                    tmk = mmop(bW2.ap[:, 0:128], SL["Wm"], SL["blk"],
                               deps=[twm, tblk] + dW)
                    bW2.w = tmk
                    emk = evac(SL["mks"], bW2.ap[:, 0:128], deps=[tmk, last["mks"]])
                    bW2.readers.append(emk)
                    last["blk"] = tmk
                    dW = bW3.wdeps()
                    tmkt = trop(bW3.ap[:, 0:128], SL["mks"], deps=[emk, t_id] + dW)
                    bW3.w = tmkt
                    emkt = evac(SL["mkts"], bW3.ap[:, 0:128], deps=[tmkt, last["mkts"]])
                    bW3.readers.append(emkt)
                    smkt = st(mktsp[vr0:vr0 + 128, :], SL["mkts"], deps=[emkt])
                    last["mkts"] = smkt
                    tk2 = vs(SL["k2t"], SL["ktok"], e2col, ALU.mult,
                             deps=[ektok, ch["e2m"], last["k2t"]])
                    sk2 = st(k2tsp[vr0:vr0 + 128, :], SL["k2t"], deps=[tk2])
                    dW = bW1.wdeps()
                    tqc = mmop(bW1.ap[:, 0:128], SL["k2t"], SL["u0s"],
                               deps=[tk2, eu0] + dW, start=True, stop=False)
                    bW1.w = tqc
                    last["u0s"] = tqc
                    dW = bW2.wdeps()
                    tkp = mmop(bW2.ap[:, 0:128], SL["k2t"], SL["mks"],
                               deps=[tk2, emk] + dW)
                    bW2.w = tkp
                    last["mks"] = tkp
                    last["k2t"] = tkp
                    ekp = evac(SL["TNp"], bW2.ap[:, 0:128], deps=[tkp, last["TNp"]])
                    bW2.readers.append(ekp)
                    last["TNp"] = ekp
                    dW = bW3.wdeps()
                    tkpt = trop(bW3.ap[:, 0:128], SL["TNp"], deps=[ekp, t_id] + dW)
                    bW3.w = tkpt
                    tpct = stt(SL["pct"], IDC, lCcol, bW3.ap[:, 0:128],
                               ALU.mult, ALU.subtract,
                               deps=[tkpt, ch["lamC"], t_id, last["pct"]])
                    bW3.readers.append(tpct)
                    ssl = Schain[:, vh * 128:(vh + 1) * 128]
                    tsc = mmop(bW1.ap[:, 0:128], SL["pct"], ssl,
                               deps=[tpct, schain_tok[vh], tqc],
                               start=False, stop=True)
                    bW1.w = tsc
                    tsn = vcopy(ssl, bW1.ap[:, 0:128], deps=[tsc])
                    bW1.readers.append(tsn)
                    schain_tok[vh] = tsn
                    psl = PROD[:, vh * 128:(vh + 1) * 128]
                    dW = bW4.wdeps()
                    tpp = mmop(bW4.ap[:, 0:128], SL["pct"], psl,
                               deps=[tpct, prod_tok[vh]] + dW)
                    bW4.w = tpp
                    epp = evac(psl, bW4.ap[:, 0:128], deps=[tpp])
                    bW4.readers.append(epp)
                    prod_tok[vh] = epp
                    last["pct"] = tpp
                    last["vtb"] = tbv
                    ch_readers += [tbl, tk2, tpct]

        cc_toks = []
        for vh in range(HV):
            psl = PROD[:, vh * 128:(vh + 1) * 128]
            dW = bW1.wdeps()
            tps = trop(bW1.ap[:, 0:128], psl, deps=[prod_tok[vh], t_id] + dW)
            bW1.w = tps
            eps_ = evac(SL["psT"], bW1.ap[:, 0:128], deps=[tps, last["psT"]])
            bW1.readers.append(eps_)
            t1 = st(ccin[(vh * 2) * 128:(vh * 2) * 128 + 128, :], SL["psT"],
                    deps=[eps_])
            last["psT"] = t1
            t2 = st(ccin[(vh * 2 + 1) * 128:(vh * 2 + 1) * 128 + 128, :],
                    Schain[:, vh * 128:(vh + 1) * 128], deps=[schain_tok[vh]])
            cc_toks += [t1, t2]
        if "ccin" in dbg_out:
            st(dbg_out["ccin"], ccin, deps=cc_toks)
        tcc = em.op("gpsimd", lambda e: e.collective_compute(
            "AllGather", ALU.bypass, replica_groups=[list(range(NC_))],
            ins=[ccin.opt() if hasattr(ccin, "opt") else ccin],
            outs=[ccout.opt() if hasattr(ccout, "opt") else ccout]),
            deps=cc_toks, inc=1)
        if "ccout" in dbg_out:
            st(dbg_out["ccout"], ccout, deps=[tcc])
        pldA = pldB = None
        s0_toks = []
        for vh in range(HV):
            s0_toks.append(vmemset(Schain[:, vh * 128:(vh + 1) * 128], 0.0,
                                   deps=[schain_tok[vh], cc_toks[-1]]))
        sst_tok = s0_toks
        isub_tok = None
        for j in range(NC_ - 1):
            mcol = CMASK[:, j:j + 1]
            ti1 = vs(SL["tmp32"], IDC, mcol, ALU.mult,
                     deps=[t_id, t_cm, last["tmp32"]])
            ti2 = vt(SL["TNp"], IDC, SL["tmp32"], ALU.subtract,
                     deps=[ti1, last["TNp"]])
            last["tmp32"], last["TNp"] = ti2, ti2
            for vh in range(HV):
                base = (j * HV * 2 + vh * 2) * 128
                l1 = ld(SL["kbuf"], ccout[base:base + 128, :],
                        deps=[tcc, last["kbuf"]])
                l2 = ld(SL["qbuf"], ccout[base + 128:base + 256, :],
                        deps=[tcc, last["qbuf"]])
                tpm = vs(SL["pct"], SL["kbuf"], mcol, ALU.mult,
                         deps=[l1, last["pct"]])
                tpm = vt(SL["pct"], SL["pct"], SL["TNp"], ALU.add,
                         deps=[tpm, ti2])
                tqm = vs(SL["blk"], SL["qbuf"], mcol, ALU.mult,
                         deps=[l2, last["blk"]])
                ssl = Schain[:, vh * 128:(vh + 1) * 128]
                dW = bW1.wdeps()
                tsm = mmop(bW1.ap[:, 0:128], SL["pct"], ssl,
                           deps=[tpm, sst_tok[vh]] + dW)
                bW1.w = tsm
                tsn = vt(ssl, bW1.ap[:, 0:128], SL["blk"], ALU.add,
                         deps=[tsm, tqm])
                bW1.readers.append(tsn)
                sst_tok[vh] = tsn
                last["kbuf"], last["qbuf"] = tsm, tsm
                last["pct"], last["blk"] = tsm, tsn
                last["TNp"] = tpm
        if "sst" in dbg_out:
            st(dbg_out["sst"], Schain, deps=sst_tok)
        return sst_tok

    def passB(sst_tok):
        names = ["u0", "mkt", "at", "k2t", "qc", "zc", "U", "O", "cgf", "hi", "dum"]
        SLs, hibs, lobs, lasts = [], [], [], []
        for pp in range(2):
            base = 4000 + pp * (len(names) * 132 + 140)
            SLs.append({n: arn(base + i * 132, 128, 128)
                        for i, n in enumerate(names)})
            hibs.append(arnb(base + len(names) * 132, 128))
            lobs.append(arnb(base + len(names) * 132 + 64, 128))
            lasts.append({n: None for n in names})
            lasts[pp]["hib"] = None
            lasts[pp]["lob"] = None
        for vh in range(HV):
            ssl = Schain[:, vh * 128:(vh + 1) * 128]
            stok = sst_tok[vh]
            for c in range(NCH):
                p_ = (vh * NCH + c) & 1
                SL, last = SLs[p_], lasts[p_]
                hib, lob = hibs[p_], lobs[p_]
                b1, b2, b3, b4 = (banks[0], banks[1], banks[2], banks[3]) if p_ == 0 \
                    else (banks[4], banks[5], banks[6], banks[7])
                vr0 = (vh * NCH + c) * 128
                qr0 = ((vh // 2) * NCH + c) * 128
                ccol = cumt[:, c * 32 + vh:c * 32 + vh + 1]
                l_u0 = ld(SL["u0"], u0sp[vr0:vr0 + 128, :], deps=[last["u0"]])
                l_mk = ld(SL["mkt"], mktsp[vr0:vr0 + 128, :], deps=[last["mkt"]])
                l_at = ld(SL["at"], atsp[vr0:vr0 + 128, :], deps=[last["at"]])
                l_k2 = ld(SL["k2t"], k2tsp[vr0:vr0 + 128, :], deps=[last["k2t"]])
                l_q = ld(SL["qc"], qsp[qr0:qr0 + 128, :], deps=[last["qc"]])
                l_z = ld(SL["zc"], zsp[vr0:vr0 + 128, :], deps=[last["zc"]])
                mb = 96 + p_ * 8
                texp = act(MISC[:, mb:mb + 1], ccol, AF.Exp, deps=[last.get("expc")])
                last["expc"] = texp
                lamCc = LAMC[:, c * 32 + vh:c * 32 + vh + 1]
                tl2 = None
                d1 = b1.wdeps()
                tmu = mmop(b1.ap[:, 0:128], SL["mkt"], ssl, deps=[l_mk, stok] + d1)
                b1.w = tmu
                tU = vt(SL["U"], SL["u0"], b1.ap[:, 0:128], ALU.subtract,
                        deps=[tmu, l_u0, last["U"]])
                b1.readers.append(tU)
                last["u0"] = tU
                last["mkt"] = tmu
                d2 = b2.wdeps()
                toq = mmop(b2.ap[:, 0:128], SL["qc"], ssl, deps=[l_q, stok] + d2)
                b2.w = toq
                d3 = b3.wdeps()
                tau = mmop(b3.ap[:, 0:128], SL["at"], SL["U"], deps=[l_at, tU] + d3)
                b3.w = tau
                eoq = evac(SL["hi"], b2.ap[:, 0:128], deps=[toq, last["hi"]])
                b2.readers.append(eoq)
                last["hi"] = eoq
                tO = stt(SL["O"], SL["hi"], MISC[:, mb:mb + 1], b3.ap[:, 0:128],
                         ALU.mult, ALU.add, deps=[eoq, tau, texp, last["O"]])
                last["expc"] = tO
                b3.readers.append(tO)
                last["at"] = tau
                last["qc"] = toq
                d1 = b1.wdeps()
                tsu = mmop(b1.ap[:, 0:128], SL["k2t"], SL["U"], deps=[l_k2, tU] + d1)
                b1.w = tsu
                tsn = stt(ssl, ssl, lamCc, b1.ap[:, 0:128], ALU.mult, ALU.add,
                          deps=[tsu, tl2, stok])
                b1.readers.append(tsn)
                stok = tsn
                last["k2t"] = tsu
                last["U"] = tsu
                tms = act(SL["dum"], SL["O"], AF.Square, deps=[tO, last["dum"]],
                          accum=MISC[:, mb + 2:mb + 3])
                trs = act(MISC[:, mb + 2:mb + 3], MISC[:, mb + 2:mb + 3], AF.Sqrt,
                          deps=[tms, e6], scale=1.0 / 128, bias=EPSC[:, 0:1])
                trc = vrecip(MISC[:, mb + 3:mb + 4], MISC[:, mb + 2:mb + 3], deps=[trs])
                tog = vs(SL["O"], SL["O"], MISC[:, mb + 3:mb + 4], ALU.mult,
                         deps=[tO, trc, tms])
                last["dum"] = tms
                d4 = b4.wdeps()
                tot = trop(b4.ap[:, 0:128], SL["O"], deps=[tog, t_id] + d4)
                b4.w = tot
                tcg = stt(SL["cgf"], b4.ap[:, 0:128], NWC[:, 0:1], SL["zc"],
                          ALU.mult, ALU.mult, deps=[tot, t_nw, l_z, last["cgf"]])
                b4.readers.append(tcg)
                last["O"] = tot
                last["zc"] = tcg
                th_ = cast(hib, SL["cgf"], deps=[tcg, last["hib"]])
                tl_ = vt(lob, SL["cgf"], hib, ALU.subtract, deps=[th_, last["lob"]])
                last["cgf"] = tl_
                s1 = st(cghsp[vh * 128:(vh + 1) * 128, c * 128:(c + 1) * 128],
                        hib, deps=[tl_])
                s2 = st(cglsp[vh * 128:(vh + 1) * 128, c * 128:(c + 1) * 128],
                        lob, deps=[tl_])
                last["hib"], last["lob"] = s1, s2

    def phase3():
        WFO = 0
        CGO = 2 * 132 * 8 + 600
        wfr = [arn(WFO + i * 260, 128, 256) for i in range(2)]
        wh_ = [arnb(WFO + 2 * 260 + i * 130, 256) for i in range(2)]
        wl_ = [arnb(WFO + 2 * 260 + 2 * 130 + i * 130, 256) for i in range(2)]
        cgh = [arnb(CGO + i * 1040, 2048) for i in range(2)]
        cgl = [arnb(CGO + 2 * 1040 + i * 1040, 2048) for i in range(2)]
        att = [arn(CGO + 4 * 1040 + 32 + i * 2064, 128, 2048) for i in range(2)]
        wtok = [None, None]
        whtok = [None, None]
        cgtok = [None, None]
        attok = [None, None]
        for mp in range(8):
            bq = banks[0:8]
            bdep = [b.wdeps() for b in bq]
            for k in range(32):
                ks = k % 2
                lw = ld(wfr[ks], woutt[k * 128:(k + 1) * 128, mp * 256:(mp + 1) * 256],
                        deps=[wtok[ks]])
                hw = cast(wh_[ks], wfr[ks], deps=[lw, whtok[ks]])
                lw2 = vt(wl_[ks], wfr[ks], wh_[ks], ALU.subtract, deps=[hw])
                wtok[ks] = lw2
                lh_ = ld(cgh[ks], cghsp[k * 128:(k + 1) * 128, :], deps=[cgtok[ks]])
                ll_ = ld(cgl[ks], cglsp[k * 128:(k + 1) * 128, :], deps=[lh_])
                for mi in range(2):
                    for si in range(4):
                        b = bq[mi * 4 + si]
                        dps = [lw2, ll_]
                        if k == 0:
                            dps += bdep[mi * 4 + si]
                        tmm = mm3op(b.ap[:, 0:512],
                                    wh_[ks][:, mi * 128:(mi + 1) * 128],
                                    wl_[ks][:, mi * 128:(mi + 1) * 128],
                                    cgh[ks][:, si * 512:(si + 1) * 512],
                                    cgl[ks][:, si * 512:(si + 1) * 512],
                                    deps=dps, start=(k == 0), stop=(k == 31))
                        if k == 31:
                            b.w = tmm
                cgtok[ks] = tmm
                whtok[ks] = tmm
            for mi in range(2):
                m = mp * 2 + mi
                evs = []
                for si in range(4):
                    b = bq[mi * 4 + si]
                    ev = evac(att[mi][:, si * 512:(si + 1) * 512], b.ap[:, 0:512],
                              deps=[b.w, attok[mi]])
                    b.readers.append(ev)
                    evs.append(ev)
                t = st(attnsp[m * 128:(m + 1) * 128, :], att[mi], deps=evs)
                attok[mi] = t
        if "attnsp" in dbg_out:
            st(dbg_out["attnsp"], attnsp, deps=[attok[0], attok[1]])
        em.barrier()
        ln_residual(attnsp, xt, x2sp, xoff=3, dbgname="x2sp")

    def ln_residual(src_sp, res_sp, dst_sp, xoff, dbgname=None, to_out=False):
        ATO = 0
        atile = [arn(ATO + i * 520, 128, 512) for i in range(16)]
        sq_ = arn(ATO + 16 * 520 + 16, 128, 512)
        xre = [arn(ATO + 16 * 520 + 544 + i * 520, 128, 512) for i in range(2)]
        at_tok = [None] * 16
        xr_tok = [None, None]
        sqg = None
        dtoks = []
        for s0 in range(0, T, 512):
            b0, b1 = banks[0], banks[1]
            d0, d1 = b0.wdeps(), b1.wdeps()
            for k in range(KT):
                lt = ld(atile[k], src_sp[k * 128:(k + 1) * 128, s0:s0 + 512],
                        deps=[at_tok[k]])
                tq = act(sq_, atile[k], AF.Square, deps=[lt, sqg])
                t1 = mmop(b0.ap[0:1, 0:512], ONEC, atile[k],
                          deps=[lt, t_onec] + (d0 if k == 0 else []),
                          start=(k == 0), stop=(k == KT - 1))
                t2 = mmop(b1.ap[0:1, 0:512], ONEC, sq_,
                          deps=[tq] + (d1 if k == 0 else []),
                          start=(k == 0), stop=(k == KT - 1))
                at_tok[k] = t1
                sqg = t2
            b0.w, b1.w = t1, t2
            ev0 = evac(ROWS[0:1, 0:512], b0.ap[0:1, 0:512], deps=[t1])
            ev1 = evac(ROWS[0:1, 1040:1552], b1.ap[0:1, 0:512], deps=[t2])
            b0.readers.append(ev0)
            b1.readers.append(ev1)
            tmu = vss(ROWS[0:1, 0:512], ROWS[0:1, 0:512], 1.0 / D, ALU.mult,
                      deps=[ev0])
            tv1 = vss(ROWS[0:1, 1040:1552], ROWS[0:1, 1040:1552], 1.0 / D,
                      ALU.mult, deps=[ev1])
            tm2 = vt(ROWS[0:1, 2080:2592], ROWS[0:1, 0:512], ROWS[0:1, 0:512],
                     ALU.mult, deps=[tmu])
            tvr = vt(ROWS[0:1, 1040:1552], ROWS[0:1, 1040:1552],
                     ROWS[0:1, 2080:2592], ALU.subtract, deps=[tv1, tm2])
            tsd = act(ROWS[0:1, 1040:1552], ROWS[0:1, 1040:1552], AF.Sqrt,
                      deps=[tvr, e5], bias=EPSC[0:1, 1:2])
            trr = vrecip(ROWS[0:1, 3120:3632], ROWS[0:1, 1040:1552],
                         deps=[tsd])
            tmr = vt(ROWS[0:1, 2080:2592], ROWS[0:1, 0:512],
                     ROWS[0:1, 3120:3632], ALU.mult, deps=[trr, tm2])
            b2, b3 = banks[2], banks[3]
            rb_t = mmop(b2.ap[:, 0:512], ONE1, ROWS[0:1, 3120:3632],
                        deps=[trr, t_one1] + b2.wdeps())
            mr_t = mmop(b3.ap[:, 0:512], ONE1, ROWS[0:1, 2080:2592],
                        deps=[tmr, t_one1] + b3.wdeps())
            b2.w, b3.w = rb_t, mr_t
            for k in range(KT):
                ks = k % 2
                if res_sp is xt:
                    lr = ld(xre[ks], res_sp[k * 128:(k + 1) * 128,
                                           xoff + s0:xoff + s0 + 512],
                            deps=[xr_tok[ks]])
                else:
                    lr = ld(xre[ks], res_sp[k * 128:(k + 1) * 128, s0:s0 + 512],
                            deps=[xr_tok[ks]])
                f1 = vt(atile[k], atile[k], b2.ap[:, 0:512], ALU.mult,
                        deps=[at_tok[k], rb_t])
                f2 = vt(atile[k], atile[k], b3.ap[:, 0:512], ALU.subtract,
                        deps=[f1, mr_t])
                f3 = vt(atile[k], atile[k], xre[ks], ALU.add, deps=[f2, lr])
                b2.readers.append(f1)
                b3.readers.append(f2)
                xr_tok[ks] = f3
                tstore = st(dst_sp[k * 128:(k + 1) * 128, s0:s0 + 512], atile[k],
                            deps=[f3])
                at_tok[k] = tstore
                dtoks.append(tstore)
        if dbgname and dbgname in dbg_out:
            st(dbg_out[dbgname], dst_sp, deps=dtoks)
        return dtoks

    def phase4():
        def xn2h(k):
            return ARNb[:, k * 512:k * 512 + 512]

        def xn2l(k):
            return ARNb[:, 8192 + k * 512:8192 + k * 512 + 512]

        P4P = 8192
        ASTH = [arnb(P4P + k * 128, 256) for k in range(KT)]
        ASTL = [arnb(P4P + 2048 + k * 128, 256) for k in range(KT)]
        BSTH = [arnb(P4P + 4096 + kb * 1024, 2048) for kb in range(2)]
        BSTL = [arnb(P4P + 6144 + kb * 1024, 2048) for kb in range(2)]
        RW1H = [arnb(P4P + 8192 + k * 32, 64) for k in range(KT)]
        RW1L = [arnb(P4P + 8704 + k * 32, 64) for k in range(KT)]
        WFFH = [arnb(P4P + 9216 + k * 128, 256) for k in range(KT)]
        WFFL = [arnb(P4P + 11264 + k * 128, 256) for k in range(KT)]
        R1 = P4P + 13312
        hsb = arn(R1, 64, 512)
        hsbh = arnb(R1 + 512, 512, p=64)
        hsbl = arnb(R1 + 768, 512, p=64)
        lgb = arn(R1 + 1024, 16, 512)
        coeff = arn(R1 + 1536, 16, 512)
        xab = [arn(R1 + 2048 + e * 512, 128, 512) for e in range(2)]
        WXH = [arnb(R1 + 3072 + e * 256, 512) for e in range(2)]
        WXL = [arnb(R1 + 3584 + e * 256, 512) for e in range(2)]
        LT = arn(R1 + 4096, 128, 64)
        x2r = [arn(R1 + 4160 + i * 512, 128, 512) for i in range(2)]
        sqr = arn(R1 + 5184, 128, 512)
        wstg = arn(R1 + 5696, 128, 256)
        fstg = [arn(R1 + 5952 + i * 512, 128, 512) for i in range(2)]

        wst_tok = None
        wg = [None]
        stat_last = None
        for k in range(KT):
            lw = ld(wstg, asttd[k * 128:(k + 1) * 128, :], deps=[wst_tok])
            hc = cast(ASTH[k], wstg, deps=[lw])
            lc = vt(ASTL[k], wstg, ASTH[k], ALU.subtract, deps=[hc])
            wst_tok = lc
        for kb in range(2):
            for hq in range(8):
                lw = ld(wstg, bsttd[kb * 128:(kb + 1) * 128, hq * 256:(hq + 1) * 256],
                        deps=[wst_tok])
                hc = cast(BSTH[kb][:, hq * 256:(hq + 1) * 256], wstg[:, 0:256],
                          deps=[lw])
                lc = vt(BSTL[kb][:, hq * 256:(hq + 1) * 256], wstg[:, 0:256],
                        BSTH[kb][:, hq * 256:(hq + 1) * 256], ALU.subtract, deps=[hc])
                wst_tok = lc
        for k in range(KT):
            lw = ld(wstg[:, 0:64], rw1td[k * 128:(k + 1) * 128, :], deps=[wst_tok])
            hc = cast(RW1H[k], wstg[:, 0:64], deps=[lw])
            lc = vt(RW1L[k], wstg[:, 0:64], RW1H[k], ALU.subtract, deps=[hc])
            wst_tok = lc
        stat_last = wst_tok

        x2_tok = [None, None]
        sqrg = [None]
        wffg = [None]
        fst_tok = [None, None]
        lt_tok = None
        cf_tok = None
        xn2_g = [stat_last]
        for q in range(4):
            s0 = q * 512
            b0, b1 = banks[0], banks[1]
            d0, d1 = b0.wdeps(), b1.wdeps()
            for k in range(KT):
                ks = k % 2
                lt = ld(x2r[ks], x2sp[k * 128:(k + 1) * 128, s0:s0 + 512],
                        deps=[x2_tok[ks]])
                tq = act(sqr, x2r[ks], AF.Square, deps=[lt, sqrg[0]])
                t1 = mmop(b0.ap[0:1, 0:512], ONEC, x2r[ks],
                          deps=[lt, t_onec] + (d0 if k == 0 else []),
                          start=(k == 0), stop=(k == KT - 1))
                t2 = mmop(b1.ap[0:1, 0:512], ONEC, sqr,
                          deps=[tq] + (d1 if k == 0 else []),
                          start=(k == 0), stop=(k == KT - 1))
                x2_tok[ks] = t1
                sqrg[0] = t2
            b0.w, b1.w = t1, t2
            ev0 = evac(ROWS[0:1, 0:512], b0.ap[0:1, 0:512], deps=[t1])
            ev1 = evac(ROWS[0:1, 1040:1552], b1.ap[0:1, 0:512], deps=[t2])
            b0.readers.append(ev0)
            b1.readers.append(ev1)
            tmu = vss(ROWS[0:1, 0:512], ROWS[0:1, 0:512], 1.0 / D, ALU.mult,
                      deps=[ev0])
            tv1 = vss(ROWS[0:1, 1040:1552], ROWS[0:1, 1040:1552], 1.0 / D,
                      ALU.mult, deps=[ev1])
            tm2 = vt(ROWS[0:1, 2080:2592], ROWS[0:1, 0:512], ROWS[0:1, 0:512],
                     ALU.mult, deps=[tmu])
            tvr = vt(ROWS[0:1, 1040:1552], ROWS[0:1, 1040:1552],
                     ROWS[0:1, 2080:2592], ALU.subtract, deps=[tv1, tm2])
            tsd = act(ROWS[0:1, 1040:1552], ROWS[0:1, 1040:1552], AF.Sqrt,
                      deps=[tvr, e5], bias=EPSC[0:1, 1:2])
            trr = vrecip(ROWS[0:1, 3120:3632], ROWS[0:1, 1040:1552], deps=[tsd])
            tmr = vt(ROWS[0:1, 2080:2592], ROWS[0:1, 0:512],
                     ROWS[0:1, 3120:3632], ALU.mult, deps=[trr, tm2])
            b2, b3 = banks[2], banks[3]
            rb_t = mmop(b2.ap[:, 0:512], ONE1, ROWS[0:1, 3120:3632],
                        deps=[trr, t_one1] + b2.wdeps())
            mr_t = mmop(b3.ap[:, 0:512], ONE1, ROWS[0:1, 2080:2592],
                        deps=[tmr, t_one1] + b3.wdeps())
            b2.w, b3.w = rb_t, mr_t
            xn2_tok = [None] * KT
            for k in range(KT):
                ks = k % 2
                lt = ld(x2r[ks], x2sp[k * 128:(k + 1) * 128, s0:s0 + 512],
                        deps=[x2_tok[ks]])
                f1 = vt(sqr, x2r[ks], b2.ap[:, 0:512], ALU.mult,
                        deps=[lt, rb_t, sqrg[0], xn2_g[0]])
                f2 = vt(sqr, sqr, b3.ap[:, 0:512], ALU.subtract, deps=[f1, mr_t])
                b2.readers.append(f1)
                b3.readers.append(f2)
                hc = cast(xn2h(k), sqr, deps=[f2])
                lc = vt(xn2l(k), sqr, xn2h(k), ALU.subtract, deps=[hc])
                x2_tok[ks] = lc
                sqrg[0] = lc
                xn2_tok[k] = lc
            xn2_last = xn2_tok[KT - 1]
            b4 = banks[4]
            d4 = b4.wdeps()
            for k in range(KT):
                tmm = mm3op(b4.ap[0:64, 0:512], RW1H[k], RW1L[k],
                            xn2h(k), xn2l(k),
                            deps=[stat_last, xn2_tok[k]] + (d4 if k == 0 else []),
                            start=(k == 0), stop=(k == KT - 1))
            b4.w = tmm
            th_ = act(hsb, b4.ap[0:64, 0:512], AF.Silu,
                      deps=[tmm, t_rb1], bias=RB1)
            b4.readers.append(th_)
            thh = cast(hsbh, hsb, deps=[th_])
            thl = vt(hsbl, hsb, hsbh, ALU.subtract, deps=[thh])
            d4 = b4.wdeps()
            tmm = mm3op(b4.ap[0:16, 0:512], RW2H, RW2L, hsbh, hsbl,
                        deps=[thl, t_rw2l] + d4, start=True, stop=True)
            b4.w = tmm
            tlg = act(lgb, b4.ap[0:16, 0:512], AF.Identity,
                      deps=[tmm, t_rbg], bias=RB2G[:, 2:3])
            b4.readers.append(tlg)
            cf_toks = []
            for gi in range(4):
                b5 = banks[5]
                d5 = b5.wdeps()
                tt1 = trop(b5.ap[:, 0:16], lgb[0:16, gi * 128:(gi + 1) * 128],
                           deps=[tlg, t_id] + d5, ident=IDC[0:16, 0:16])
                b5.w = tt1
                te = evac(LT[:, 0:16], b5.ap[:, 0:16], deps=[tt1, lt_tok])
                b5.readers.append(te)
                tm1 = vrmax(MISC[:, 100:101], LT[:, 0:16], deps=[te, lt_tok])
                tms1 = vs(LT[:, 16:32], LT[:, 0:16], MISC[:, 100:101], ALU.is_ge,
                          deps=[tm1])
                tlm = stt(LT[:, 32:48], LT[:, 16:32], -1e30, LT[:, 0:16],
                          ALU.mult, ALU.add, deps=[tms1])
                tm2_ = vrmax(MISC[:, 101:102], LT[:, 32:48], deps=[tlm])
                tms2 = vs(LT[:, 32:48], LT[:, 32:48], MISC[:, 101:102], ALU.is_ge,
                          deps=[tm2_])
                tdd = vt(MISC[:, 102:103], MISC[:, 100:101], MISC[:, 101:102],
                         ALU.subtract, deps=[tm2_, tm1])
                ts1 = act(MISC[:, 103:104], MISC[:, 102:103], AF.Sigmoid, deps=[tdd])
                ts2 = act(MISC[:, 104:105], MISC[:, 102:103], AF.Sigmoid,
                          deps=[tdd], scale=-1.0)
                tc1 = vs(LT[:, 16:32], LT[:, 16:32], MISC[:, 103:104], ALU.mult,
                         deps=[ts1, tms1])
                tc2 = vs(LT[:, 32:48], LT[:, 32:48], MISC[:, 104:105], ALU.mult,
                         deps=[ts2, tms2])
                tcf = vt(LT[:, 16:32], LT[:, 16:32], LT[:, 32:48], ALU.add,
                         deps=[tc1, tc2])
                lt_tok = tcf
                b6 = banks[6]
                d6 = b6.wdeps()
                tt2 = trop(b6.ap[0:16, 0:128], LT[:, 16:32], deps=[tcf, t_id] + d6)
                b6.w = tt2
                tev = evac(coeff[:, gi * 128:(gi + 1) * 128], b6.ap[0:16, 0:128],
                           deps=[tt2, cf_tok])
                b6.readers.append(tev)
                cf_toks.append(tev)
                cf_tok = tev
            wx_toks = []
            for e2 in range(2):
                b7 = banks[7]
                d7 = b7.wdeps()
                for k in range(KT):
                    tmm = mm3op(b7.ap[:, 0:512],
                                ASTH[k][:, e2 * 128:e2 * 128 + 128],
                                ASTL[k][:, e2 * 128:e2 * 128 + 128],
                                xn2h(k), xn2l(k),
                                deps=[stat_last, xn2_tok[k]] + (d7 if k == 0 else []),
                                start=(k == 0), stop=(k == KT - 1))
                b7.w = tmm
                exa = evac(xab[e2], b7.ap[:, 0:512], deps=[tmm, wx_toks[-1] if wx_toks else None])
                b7.readers.append(exa)
                b6 = banks[6]
                d6 = b6.wdeps()
                tcr = mmop(b6.ap[:, 0:512], REPT[:, e2 * 128:(e2 + 1) * 128],
                           coeff, deps=cf_toks + [t_rept] + d6)
                b6.w = tcr
                twx = vt(xab[e2], xab[e2], b6.ap[:, 0:512], ALU.mult,
                         deps=[exa, tcr])
                b6.readers.append(twx)
                thc = cast(WXH[e2], xab[e2], deps=[twx])
                tlc = vt(WXL[e2], xab[e2], WXH[e2], ALU.subtract, deps=[thc])
                wx_toks.append(tlc)
            for mp in range(8):
                bqa, bqb = banks[(2 * mp) % 8], banks[(2 * mp + 1) % 8]
                bd = [bqa.wdeps(), bqb.wdeps()]
                for k in range(KT):
                    lw = ld(wstg, wffnt[k * 128:(k + 1) * 128, mp * 256:(mp + 1) * 256],
                            deps=[wffg[0]])
                    hc = cast(WFFH[k], wstg, deps=[lw, wffg[0]])
                    lc = vt(WFFL[k], wstg, WFFH[k], ALU.subtract, deps=[hc])
                    wffg[0] = lc
                    for mi, b in enumerate((bqa, bqb)):
                        tmm = mm3op(b.ap[:, 0:512],
                                    WFFH[k][:, mi * 128:mi * 128 + 128],
                                    WFFL[k][:, mi * 128:mi * 128 + 128],
                                    xn2h(k), xn2l(k),
                                    deps=[lc, xn2_tok[k]] + (bd[mi] if k == 0 else []),
                                    start=(k == 0), stop=False)
                for kb in range(2):
                    for mi, b in enumerate((bqa, bqb)):
                        mcol = mp * 256 + mi * 128
                        tmm = mm3op(b.ap[:, 0:512],
                                    BSTH[kb][:, mcol:mcol + 128],
                                    BSTL[kb][:, mcol:mcol + 128],
                                    WXH[kb], WXL[kb],
                                    deps=wx_toks + [stat_last], start=False,
                                    stop=(kb == 1))
                        if kb == 1:
                            b.w = tmm
                wffg[0] = tmm
                for mi, b in enumerate((bqa, bqb)):
                    m = mp * 2 + mi
                    ev = act(fstg[mi], b.ap[:, 0:512], AF.Identity,
                             deps=[b.w, fst_tok[mi], t_fb], bias=FB[:, m:m + 1])
                    b.readers.append(ev)
                    tst = st(ffnsp[m * 128:(m + 1) * 128, s0:s0 + 512], fstg[mi],
                             deps=[ev])
                    fst_tok[mi] = tst
            xn2_g[0] = tmm
        if "ffnsp" in dbg_out:
            st(dbg_out["ffnsp"], ffnsp, deps=[fst_tok[0], fst_tok[1]])
        em.barrier()
        ln_residual(ffnsp, x2sp, outt, xoff=0)

    sst = None
    if "p1" in phases:
        phase1()
        em.barrier()
    if "pa" in phases:
        sst = passA()
        em.barrier()
    if "pb" in phases:
        passB(sst)
        em.barrier()
    if "p3" in phases:
        phase3()
        em.barrier()
    if "p4" in phases:
        phase4()
    em.barrier()
    for name in dbg:
        if name in dbg_out and name in dbg_specs and name not in (
                "ccin", "ccout", "attnsp", "x2sp", "ffnsp"):
            st(dbg_out[name], dbg_specs[name][0], deps=em.dma_counts())
    if "vsp2" in dbg_out:
        st(dbg_out["vsp2"], dbg_specs["vsp"][0], deps=em.dma_counts())
    em.barrier()
    em.finalize()
    return nc


def stage_inputs(inputs):
    x = np.asarray(inputs["L0_attn_hidden_states"], np.float32)[0]
    Wq = np.ascontiguousarray(np.asarray(inputs["L0_attn_W_qkvz"], np.float32))
    Wba = np.asarray(inputs["L0_attn_W_ba"], np.float32)
    convw = np.asarray(inputs["L0_attn_conv_weight"], np.float32)
    alog = np.asarray(inputs["L0_attn_A_log"], np.float32)
    dtb = np.asarray(inputs["L0_attn_dt_bias"], np.float32)
    nw = np.asarray(inputs["L0_attn_norm_weight"], np.float32)
    Wout = np.ascontiguousarray(np.asarray(inputs["L0_attn_W_out"], np.float32))
    W = np.asarray(inputs["L0_ffn_W"], np.float32)
    bias = np.asarray(inputs["L0_ffn_bias"], np.float32)
    rw1 = np.asarray(inputs["L0_ffn_rw1"], np.float32)
    rb1 = np.asarray(inputs["L0_ffn_rb1"], np.float32)
    rw2 = np.asarray(inputs["L0_ffn_rw2"], np.float32)
    rb2 = np.asarray(inputs["L0_ffn_rb2"], np.float32)
    A = np.asarray(inputs["L0_ffn_A"], np.float32)
    Bm = np.asarray(inputs["L0_ffn_B"], np.float32)
    gates = np.asarray(inputs["L0_ffn_gates"], np.float32)

    xT = np.ascontiguousarray(x.T)
    perm = np.zeros(64, np.int64)
    for h in range(16):
        for j in range(2):
            perm[2 * h + j] = 4 * h + j
            perm[32 + 2 * h + j] = 4 * h + 2 + j
    wbat = np.ascontiguousarray(Wba[:, perm])
    cwt = np.zeros((64 * 128, 4), np.float32)
    for ci in range(64):
        if ci < 16:
            rows = np.arange(ci * 128, ci * 128 + 128)
        elif ci < 32:
            rows = np.arange(2048 + (ci - 16) * 128, 2048 + (ci - 16) * 128 + 128)
        else:
            rows = np.arange(4096 + (ci - 32) * 128, 4096 + (ci - 32) * 128 + 128)
        cwt[ci * 128:(ci + 1) * 128] = convw[rows]
    smallp = np.zeros((64, 2), np.float32)
    smallp[32:64, 0] = alog
    smallp[32:64, 1] = dtb
    nwc = nw.reshape(128, 1).astype(np.float32)
    wffnt = np.ascontiguousarray(W.T)
    fbias = np.ascontiguousarray(bias.reshape(16, 128).T)
    rw1t = np.ascontiguousarray(rw1.T)
    rw2t = np.ascontiguousarray(rw2[:16].T)
    astt = np.ascontiguousarray(A.reshape(256, D).T)
    bstt = np.ascontiguousarray(Bm.transpose(0, 2, 1).reshape(256, D))
    rept = np.zeros((16, 256), np.float32)
    for e in range(16):
        rept[e, e * 16:(e + 1) * 16] = 1.0
    ident = np.eye(128, dtype=np.float32)
    ii = np.arange(128)
    ml = np.where(ii[None, :] < ii[:, None], 0.0, -1e30).astype(np.float32)
    mu = np.where(ii[None, :] > ii[:, None], 0.0, -1e30).astype(np.float32)
    mui = np.where(ii[None, :] >= ii[:, None], 0.0, -1e30).astype(np.float32)
    masks = np.concatenate([ml, mu, mui], axis=1)

    common = dict(
        wqkvz=Wq, wbat=wbat, convwt=cwt, smallp=smallp, nwcd=nwc,
        woutt=Wout, wffnt=wffnt, fbiast=fbias, rw1td=rw1t,
        rb1cd=rb1.reshape(64, 1), rw2td=rw2t, rb2cd=rb2[:16].reshape(16, 1),
        gatesd=gates.reshape(16, 1), asttd=astt, bsttd=bstt, reptd=rept,
        identd=ident, masksd=masks,
    )
    in_maps = []
    for ci in range(NC_):
        s0 = ci * T
        xtc = np.zeros((D, 3 + T), np.float32)
        lo = max(0, s0 - 3)
        nh = s0 - lo
        if nh:
            xtc[:, 3 - nh:3] = xT[:, lo:s0]
        xtc[:, 3:] = xT[:, s0:s0 + T]
        cm = np.zeros((128, 8), np.float32)
        cm[:, :ci] = 1.0
        m = dict(common)
        m["xt"] = xtc
        m["cmaskd"] = cm
        in_maps.append(m)
    return in_maps


def kernel(**inputs):
    in_maps = stage_inputs(inputs)
    nc = build()
    res = run_bass_kernel_spmd(nc, in_maps, core_ids=list(range(NC_)))
    outs = []
    for ci in range(NC_):
        outs.append(np.asarray(res.results[ci]["outt"]).T)
    y = np.concatenate(outs, axis=0)[None, :, :].astype(np.float32)
    return y


# version 06 (final)
# speedup vs baseline: 1.0498x; 1.0498x over previous
import numpy as np
import concourse.bass as bass
import concourse.mybir as mybir
from concourse.bass_utils import run_bass_kernel_spmd

F32 = mybir.dt.float32
BF16 = mybir.dt.bfloat16
AF = mybir.ActivationFunctionType
ALU = mybir.AluOpType

NC_ = 8
T = 2048
D = 2048
KT = 16
HK = 16
HV = 32
C = 128
NCH = 16
HALF = 1024
XW = HALF + 3
EPS6 = 1e-6
EPS5 = 1e-5

ENGS = ("sync", "tensor", "vector", "scalar", "gpsimd")


NLANE = 24


class Em:

    def __init__(self, nc):
        self.nc = nc
        self.sems = {e: nc.alloc_semaphore("s_" + e) for e in ENGS}
        for q in ("sync", "gpsimd"):
            for l in range(NLANE):
                self.sems[f"{q}.{l}"] = nc.alloc_semaphore(f"d_{q}{l}")
        self.cnt = {e: 0 for e in ENGS}
        self.dcnt = {"sync": 0, "gpsimd": 0}
        self.th = {e: [] for e in ENGS}
        self.wm = {e: {} for e in ENGS}

    def _waits(self, eng, deps):
        need = {}
        for d in deps:
            if d is None:
                continue
            se, c = d
            if c > self.wm[eng].get(se, 0):
                need[se] = max(need.get(se, 0), c)
        for se, c in sorted(need.items()):
            self.wm[eng][se] = c
            s = self.sems[se]
            self.th[eng].append(lambda e, s=s, c=c: e.wait_ge(s, c))

    def op(self, eng, fn, deps=(), inc=1):
        self._waits(eng, deps)
        self.cnt[eng] += inc
        tok = (eng, self.cnt[eng])
        s = self.sems[eng]
        self.th[eng].append(lambda e, fn=fn, s=s, i=inc: fn(e).then_inc(s, i))
        return tok

    def dma(self, q, fn, deps=()):
        i = self.dcnt[q]
        lane = i % NLANE
        sname = f"{q}.{lane}"
        deps = list(deps)
        if i >= NLANE:
            deps.append((sname, 16 * (i // NLANE)))
        self._waits(q, deps)
        self.dcnt[q] = i + 1
        tok = (sname, 16 * (i // NLANE + 1))
        s = self.sems[sname]
        self.th[q].append(lambda e, fn=fn, s=s: fn(e).then_inc(s, 16))
        return tok

    def dma_counts(self):
        out = []
        for q in ("sync", "gpsimd"):
            n = self.dcnt[q]
            for l in range(NLANE):
                cnt = (n - l + NLANE - 1) // NLANE if n > l else 0
                if cnt > 0:
                    out.append((f"{q}.{l}", 16 * cnt))
        return out

    def barrier(self):
        toks = [(e, self.cnt[e]) for e in ENGS if self.cnt[e] > 0]
        toks += self.dma_counts()
        for e in ENGS:
            need = [(se, c) for se, c in toks
                    if se != e and c > self.wm[e].get(se, 0)]
            for se, c in need:
                self.wm[e][se] = c
                s = self.sems[se]
                self.th[e].append(lambda en, s=s, c=c: en.wait_ge(s, c))

    def finalize(self):
        nc, th = self.nc, self.th
        with nc.Block() as block:
            @block.sync
            def _(e):
                for f in th["sync"]:
                    f(e)

            @block.tensor
            def _(e):
                for f in th["tensor"]:
                    f(e)

            @block.vector
            def _(e):
                for f in th["vector"]:
                    f(e)

            @block.scalar
            def _(e):
                for f in th["scalar"]:
                    f(e)

            @block.gpsimd
            def _(e):
                for f in th["gpsimd"]:
                    f(e)


class Bank:
    def __init__(self, ap):
        self.ap = ap
        self.w = None
        self.readers = []

    def wdeps(self):
        d = list(self.readers)
        if self.w is not None:
            d.append(self.w)
        self.readers = []
        return d


def build(phases=("p1", "pa", "pb", "p3", "p4"), dbg=()):
    nc = bass.Bass("TRN2", target_bir_lowering=False, debug=False)
    em = Em(nc)

    def din(name, shape, dt=F32):
        return nc.dram_tensor(name, list(shape), dt, kind="ExternalInput").ap()

    xt = din("xt", (D, 3 + T))
    wqkvz = din("wqkvz", (D, 12288))
    wbat = din("wbat", (D, 64))
    convwt = din("convwt", (64 * 128, 4))
    smallp = din("smallp", (64, 2))
    nwcd = din("nwcd", (128, 1))
    woutt = din("woutt", (4096, D))
    wffnt = din("wffnt", (D, D))
    fbiast = din("fbiast", (128, 16))
    rw1td = din("rw1td", (D, 64))
    rb1cd = din("rb1cd", (64, 1))
    rw2td = din("rw2td", (64, 16))
    rb2cd = din("rb2cd", (16, 1))
    gatesd = din("gatesd", (16, 1))
    asttd = din("asttd", (D, 256))
    bsttd = din("bsttd", (256, D))
    reptd = din("reptd", (16, 256))
    identd = din("identd", (128, 128))
    masksd = din("masksd", (128, 384))
    cmaskd = din("cmaskd", (128, 8))

    outt = nc.dram_tensor("outt", [D, T], F32, kind="ExternalOutput").ap()

    def dint(name, shape, dt=F32):
        return nc.dram_tensor(name, list(shape), dt).ap()

    qsp = dint("qsp", (HK * NCH * 128, 128))
    ksp = dint("ksp", (HK * NCH * 128, 128))
    vsp = dint("vsp", (HV * NCH * 128, 128))
    zsp = dint("zsp", (HV * NCH * 128, 128))
    u0sp = dint("u0sp", (HV * NCH * 128, 128))
    mktsp = dint("mktsp", (HV * NCH * 128, 128))
    atsp = dint("atsp", (HV * NCH * 128, 128))
    k2tsp = dint("k2tsp", (HV * NCH * 128, 128))
    cghsp = dint("cghsp", (HV * 128, T), BF16)
    cglsp = dint("cglsp", (HV * 128, T), BF16)
    attnsp = dint("attnsp", (D, T))
    x2sp = dint("x2sp", (D, T))
    ffnsp = dint("ffnsp", (D, T))
    ccin = dint("ccin", (HV * 2 * 128, 128))
    ccout = dint("ccout", (NC_ * HV * 2 * 128, 128))

    dbg_specs = {
        "qsp": (qsp, (HK * NCH * 128, 128)), "ksp": (ksp, (HK * NCH * 128, 128)),
        "vsp": (vsp, (HV * NCH * 128, 128)), "zsp": (zsp, (HV * NCH * 128, 128)),
        "u0sp": (u0sp, (HV * NCH * 128, 128)), "mktsp": (mktsp, (HV * NCH * 128, 128)),
        "atsp": (atsp, (HV * NCH * 128, 128)), "k2tsp": (k2tsp, (HV * NCH * 128, 128)),
        "attnsp": (attnsp, (D, T)), "x2sp": (x2sp, (D, T)), "ffnsp": (ffnsp, (D, T)),
        "ccin": (ccin, (HV * 2 * 128, 128)), "ccout": (ccout, (NC_ * HV * 2 * 128, 128)),
    }
    dbg_out = {}
    for name in dbg:
        if name in dbg_specs:
            dbg_out[name] = nc.dram_tensor(
                "dbg_" + name, list(dbg_specs[name][1]), F32, kind="ExternalOutput").ap()
    if "vsp2" in dbg:
        dbg_out["vsp2"] = nc.dram_tensor(
            "dbg_vsp2", list(dbg_specs["vsp"][1]), F32, kind="ExternalOutput").ap()
    want_sb_dbg = [n for n in dbg if n in ("babuf", "cumh", "cumt", "btt", "sst")]
    for name in want_sb_dbg:
        shp = {"babuf": (64, T), "cumh": (32, T), "cumt": (128, 512),
               "btt": (128, 512), "sst": (128, HV * 128)}[name]
        dbg_out[name] = nc.dram_tensor("dbg_" + name, list(shp), F32,
                                       kind="ExternalOutput").ap()

    def sbt(name, p, w, dt=F32):
        return nc.alloc_sbuf_tensor(name, [p, w], dt).ap()

    IDC = sbt("IDC", 128, 128)
    MSK = sbt("MSK", 128, 384)
    ONE1 = sbt("ONE1", 1, 128)
    ONEC = sbt("ONEC", 128, 1)
    NEG1 = sbt("NEG1", 128, 1)
    EPSC = sbt("EPSC", 128, 2)
    CMASK = sbt("CMASK", 128, 8)
    NWC = sbt("NWC", 128, 1)
    SMALLP = sbt("SMALLP", 64, 4)
    FB = sbt("FB", 128, 16)
    RB1 = sbt("RB1", 64, 1)
    RB2G = sbt("RB2G", 16, 3)
    RW2 = sbt("RW2", 64, 16)
    RW2H = sbt("RW2H", 64, 16, BF16)
    RW2L = sbt("RW2L", 64, 16, BF16)
    REPT = sbt("REPT", 16, 256)
    ONESF = sbt("ONESF", 128, 128)
    ONES32 = sbt("ONES32", 32, 128)
    CW = sbt("CW", 128, 256)
    ROWS = sbt("ROWS", 1, 4160)
    LAMC = sbt("LAMC", 128, NCH * 32)
    babuf = sbt("babuf", 64, T)
    cumh = sbt("cumh", 32, T)
    cumt = sbt("cumt", 128, NCH * 32)
    btt = sbt("btt", 128, NCH * 32)
    nbtt = sbt("nbtt", 128, NCH * 32)
    MISC = sbt("MISC", 128, 160)
    Schain = sbt("Schain", 128, HV * 128)
    PROD = sbt("PROD", 128, HV * 128)
    WBAH = sbt("WBAH", 128, KT * 64, BF16)
    WBAL = sbt("WBAL", 128, KT * 64, BF16)
    STASH = sbt("STASH", 128, 64 * 4)
    ARN = nc.alloc_sbuf_tensor("ARN", [128, 31744], F32)
    ARNa = ARN.ap()
    ARNb = ARN.ap().bitcast(BF16)

    def arn(off, p, w):
        return ARNa[0:p, off:off + w]

    def arnb(off_f32, w_bf16, p=128):
        return ARNb[0:p, 2 * off_f32:2 * off_f32 + w_bf16]

    psum = [nc.alloc_psum_tensor(f"pbk{i}", [128, 512], F32) for i in range(8)]
    banks = [Bank(p.ap()) for p in psum]

    def ld(dst, src, deps=()):
        return em.dma("sync", lambda e, d=dst, s=src: e.dma_start(out=d, in_=s),
                      deps=deps)

    def st(dst, src, deps=()):
        return em.dma("gpsimd", lambda e, d=dst, s=src: e.dma_start(out=d, in_=s),
                      deps=deps)

    def cast(dst, src, deps=()):
        return em.dma("gpsimd", lambda e, d=dst, s=src: e.dma_start(out=d, in_=s),
                      deps=deps)

    def mmop(out, lhsT, rhs, deps=(), start=True, stop=True):
        return em.op("tensor", lambda e, o=out, l=lhsT, r=rhs, a=start, z=stop:
                     e.matmul(o, l, r, start=a, stop=z), deps=deps)

    def mm3op(out, lh, ll, rh, rl, deps=(), start=True, stop=True):
        em.op("tensor", lambda e: e.matmul(out, lh, rh, start=start, stop=False),
              deps=deps)
        em.op("tensor", lambda e: e.matmul(out, lh, rl, start=False, stop=False))
        return em.op("tensor", lambda e: e.matmul(out, ll, rh, start=False, stop=stop))

    def trop(out, src, deps=(), ident=None):
        idn = ident if ident is not None else IDC
        return em.op("tensor", lambda e, o=out, s=src, i=idn: e.transpose(o, s, i),
                     deps=deps)

    def evac(dst, src, deps=()):
        return em.op("scalar", lambda e, d=dst, s=src: e.copy(d, s), deps=deps)

    def act(out, in_, func, deps=(), bias=None, scale=None, accum=None):
        kw = {}
        if bias is not None:
            kw["bias"] = bias
        if scale is not None:
            kw["scale"] = scale
        if accum is not None:
            kw["accum_out"] = accum
        return em.op("scalar", lambda e, o=out, i=in_, f=func, k=kw:
                     e.activation(o, i, f, **k), deps=deps)

    def vt(out, a, b, opx, deps=()):
        return em.op("vector", lambda e: e.tensor_tensor(out, a, b, op=opx),
                     deps=deps)

    def vs(out, a, col, opx, deps=(), col2=None, op2=None):
        if col2 is not None:
            return em.op("vector", lambda e: e.tensor_scalar(
                out, a, col, col2, op0=opx, op1=op2), deps=deps)
        return em.op("vector", lambda e: e.tensor_scalar(
            out, a, col, None, op0=opx), deps=deps)

    def stt(out, in0, scl, in1, op0, op1, deps=()):
        return em.op("vector", lambda e: e.scalar_tensor_tensor(
            out, in0, scl, in1, op0=op0, op1=op1), deps=deps)

    def vss(out, a, const, opx, deps=()):
        return em.op("vector", lambda e: e.tensor_single_scalar(
            out, a, const, op=opx), deps=deps)

    def vcopy(out, a, deps=()):
        return em.op("vector", lambda e: e.tensor_copy(out, a), deps=deps)

    def vmemset(ap_, c, deps=()):
        return em.op("vector", lambda e: e.memset(ap_, c), deps=deps)

    def vrecip(out, a, deps=()):
        return em.op("vector", lambda e: e.reciprocal(out, a), deps=deps)

    def vrmax(out, a, deps=()):
        return em.op("vector", lambda e: e.reduce_max(out, a, axis=mybir.AxisListType.X),
                     deps=deps)

    t_id = ld(IDC, identd)
    t_msk = ld(MSK, masksd)
    t_cm = ld(CMASK, cmaskd)
    t_nw = ld(NWC, nwcd)
    t_sp = ld(SMALLP[:, 0:2], smallp)
    t_fb = ld(FB, fbiast)
    t_rb1 = ld(RB1, rb1cd)
    t_rb2 = ld(RB2G[:, 0:1], rb2cd)
    t_gt = ld(RB2G[:, 1:2], gatesd)
    t_rw2 = ld(RW2, rw2td)
    t_rept = ld(REPT, reptd)
    cw_toks = [ld(CW[:, ci * 4:(ci + 1) * 4], convwt[ci * 128:(ci + 1) * 128, :])
               for ci in range(64)]
    t_one1 = vmemset(ONE1, 1.0)
    t_onec = vmemset(ONEC, 1.0)
    t_neg1 = vmemset(NEG1, -1.0)
    t_onesf = vmemset(ONESF, 1.0)
    t_ones32 = vmemset(ONES32, 1.0)
    e6 = vmemset(EPSC[:, 0:1], EPS6)
    e5 = vmemset(EPSC[:, 1:2], EPS5)
    tk = act(SMALLP[32:64, 2:3], SMALLP[32:64, 0:1], AF.Exp, deps=[t_sp])
    t_nea = vs(SMALLP[32:64, 2:3], SMALLP[32:64, 2:3], NEG1[32:64, :], ALU.mult,
               deps=[tk, t_neg1])
    t_rbg = vt(RB2G[:, 2:3], RB2G[:, 0:1], RB2G[:, 1:2], ALU.add,
               deps=[t_rb2, t_gt])
    t_rw2h = cast(RW2H, RW2, deps=[t_rw2])
    t_rw2l = vt(RW2L, RW2, RW2H, ALU.subtract, deps=[t_rw2h])

    XN_BF_STRIDE = 1040
    XN_LO_BASE = KT * XN_BF_STRIDE
    XST_O = KT * XN_BF_STRIDE + 32
    RAW_O = XST_O + 2 * 512 + 16
    CNV_O = RAW_O + 2 * XW + 16
    WF_O = CNV_O + 2 * XW + 16
    WH_O = WF_O + 4 * 256 + 16
    SQ_O = WH_O + KT * 256 + 16
    XAP_O = SQ_O + XW + 16

    def conv_ci(m):
        h, sub = m // 6, m % 6
        if sub >= 4:
            return None
        if sub == 0:
            return h
        if sub == 1:
            return 16 + h
        return 32 + 2 * h + (sub - 2)

    def spill_dst(m):
        h, sub = m // 6, m % 6
        if sub == 0:
            return qsp, h
        if sub == 1:
            return ksp, h
        if sub in (2, 3):
            return vsp, 2 * h + (sub - 2)
        return zsp, 2 * h + (sub - 4)

    def phase1():
        xstg = [arn(XST_O + i * 512, 128, 512) for i in range(2)]
        raws = [arn(RAW_O + i * XW, 128, XW) for i in range(2)]
        cnvs = [arn(CNV_O + i * XW, 128, XW) for i in range(2)]
        wf = [arn(WF_O + i * 256, 128, 256) for i in range(4)]
        whi = [arnb(WH_O + k * 256, 256) for k in range(KT)]
        wlo = [arnb(WH_O + k * 256 + 128, 256) for k in range(KT)]
        sqs = arn(SQ_O, 128, XW)
        xap = [arn(XAP_O + i * 512, 128, 512) for i in range(2)]
        xnf = [arn(XAP_O + 1024 + 16 + i * 512, 128, 512) for i in range(2)]
        x_tok = [None, None]
        xa_tok = [None, None]
        xf_tok = [None, None]
        wf_tok = [None, None, None, None]
        wh_tok = [None] * KT
        raw_tok = [None, None]
        cnv_tok = [None, None]
        stash_toks = {}
        wba_stg = arn(SQ_O, 128, KT * 64)
        t_wba = None
        for k in range(KT):
            t_wba = ld(wba_stg[:, k * 64:(k + 1) * 64],
                       wbat[k * 128:(k + 1) * 128, :], deps=[])
        t_wbah = cast(WBAH, wba_stg, deps=[t_wba])
        t_wbal = vt(WBAL, wba_stg, WBAH, ALU.subtract, deps=[t_wbah])
        sqs_g = [t_wbal]

        for half in range(2):
            hb = half * HALF
            xoff = 0 if half == 0 else XW
            ncol = XW if half == 0 else HALF
            stripes = [(0, 512), (512, 512), (1024, 3)] if half == 0 else \
                      [(0, 512), (512, 512)]

            def xnh(k, s0, w):
                b0_ = k * XN_BF_STRIDE + s0
                return ARNb[:, b0_:b0_ + w]

            def xnl(k, s0, w):
                b0_ = XN_LO_BASE + k * XN_BF_STRIDE + s0
                return ARNb[:, b0_:b0_ + w]

            rsum = lambda s0, w: ROWS[0:1, s0:s0 + w]
            rsq = lambda s0, w: ROWS[0:1, 1040 + s0:1040 + s0 + w]
            rw1 = lambda s0, w: ROWS[0:1, 2080 + s0:2080 + s0 + w]
            rw2 = lambda s0, w: ROWS[0:1, 3120 + s0:3120 + s0 + w]

            ev0 = ev1 = None
            for s0, w in stripes:
                b0, b1 = banks[0], banks[1]
                d0, d1 = b0.wdeps(), b1.wdeps()
                for k in range(KT):
                    slot = k % 2
                    xl = ld(xstg[slot][:, 0:w],
                            xt[k * 128:(k + 1) * 128, xoff + s0:xoff + s0 + w],
                            deps=[x_tok[slot]])
                    sq = act(sqs[:, 0:w], xstg[slot][:, 0:w], AF.Square,
                             deps=[xl, sqs_g[0]])
                    t1 = mmop(b0.ap[0:1, 0:w], ONEC, xstg[slot][:, 0:w],
                              deps=[xl, t_onec] + (d0 if k == 0 else []),
                              start=(k == 0), stop=(k == KT - 1))
                    t2 = mmop(b1.ap[0:1, 0:w], ONEC, sqs[:, 0:w],
                              deps=[sq] + (d1 if k == 0 else []),
                              start=(k == 0), stop=(k == KT - 1))
                    x_tok[slot] = t2
                    sqs_g[0] = t2
                b0.w, b1.w = t1, t2
                ev0 = evac(rsum(s0, w), b0.ap[0:1, 0:w], deps=[t1])
                ev1 = evac(rsq(s0, w), b1.ap[0:1, 0:w], deps=[t2])
                b0.readers.append(ev0)
                b1.readers.append(ev1)
            tmu = vss(rsum(0, ncol), rsum(0, ncol), 1.0 / D, ALU.mult, deps=[ev0])
            tv1 = vss(rsq(0, ncol), rsq(0, ncol), 1.0 / D, ALU.mult, deps=[ev1])
            tm2 = vt(rw1(0, ncol), rsum(0, ncol), rsum(0, ncol), ALU.mult,
                     deps=[tmu])
            tvr = vt(rsq(0, ncol), rsq(0, ncol), rw1(0, ncol), ALU.subtract,
                     deps=[tv1, tm2])
            tsd = act(rsq(0, ncol), rsq(0, ncol), AF.Sqrt, deps=[tvr, e5],
                      bias=EPSC[0:1, 1:2])
            trr = vrecip(rw2(0, ncol), rsq(0, ncol), deps=[tsd])
            tmr = vt(rw1(0, ncol), rsum(0, ncol), rw2(0, ncol), ALU.mult,
                     deps=[trr, tm2])
            for s0, w in stripes:
                b2, b3 = banks[2], banks[3]
                rb_t = mmop(b2.ap[:, 0:w], ONE1, rw2(s0, w),
                            deps=[trr, t_one1] + b2.wdeps())
                mr_t = mmop(b3.ap[:, 0:w], ONE1, rw1(s0, w),
                            deps=[tmr, t_one1] + b3.wdeps())
                b2.w, b3.w = rb_t, mr_t
                for k in range(KT):
                    slot = k % 2
                    xl = ld(xap[slot][:, 0:w],
                            xt[k * 128:(k + 1) * 128, xoff + s0:xoff + s0 + w],
                            deps=[xa_tok[slot]])
                    f1 = vt(xnf[slot][:, 0:w], xap[slot][:, 0:w], b2.ap[:, 0:w],
                            ALU.mult, deps=[xl, rb_t, xf_tok[slot]])
                    f2 = vt(xnf[slot][:, 0:w], xnf[slot][:, 0:w], b3.ap[:, 0:w],
                            ALU.subtract, deps=[f1, mr_t])
                    b2.readers.append(f1)
                    b3.readers.append(f2)
                    hc = cast(xnh(k, s0, w), xnf[slot][:, 0:w], deps=[f2])
                    lc = vt(xnl(k, s0, w), xnf[slot][:, 0:w], xnh(k, s0, w),
                            ALU.subtract, deps=[hc])
                    xa_tok[slot] = f2
                    xf_tok[slot] = lc
            xn_done = [xf_tok[0], xf_tok[1]]

            for mg in range(48):
                m0 = 2 * mg
                qb = [banks[2 + i] for i in range(6)]
                qdeps = [b.wdeps() for b in qb]
                wl_toks = [None] * KT
                for k in range(KT):
                    slot = k % 4
                    wl = ld(wf[slot], wqkvz[k * 128:(k + 1) * 128, m0 * 128:(m0 + 2) * 128],
                            deps=[wf_tok[slot]])
                    wh_ = cast(whi[k], wf[slot], deps=[wl, wh_tok[k]])
                    wl_ = vt(wlo[k], wf[slot], whi[k], ALU.subtract, deps=[wh_])
                    wf_tok[slot] = wl_
                    wl_toks[k] = wl_
                for mi in range(2):
                    for si, (s0, w) in enumerate(stripes):
                        b = qb[mi * 3 + si]
                        for k in range(KT):
                            dps = [wl_toks[k], xn_done[0], xn_done[1]]
                            if k == 0:
                                dps += qdeps[mi * 3 + si]
                            tmm = mm3op(b.ap[:, 0:w],
                                        whi[k][:, mi * 128:(mi + 1) * 128],
                                        wlo[k][:, mi * 128:(mi + 1) * 128],
                                        xnh(k, s0, w), xnl(k, s0, w),
                                        deps=dps, start=(k == 0), stop=(k == KT - 1))
                        b.w = tmm
                for kk in range(KT):
                    wh_tok[kk] = tmm
                for mi in range(2):
                    m = m0 + mi
                    ci = conv_ci(m)
                    rw_ = raws[mi]
                    cv_ = cnvs[mi]
                    evs = []
                    for si, (s0, w) in enumerate(stripes):
                        b = qb[mi * 3 + si]
                        dst_off = s0 if half == 0 else 3 + s0
                        ev = evac(rw_[:, dst_off:dst_off + w], b.ap[:, 0:w],
                                  deps=[b.w, raw_tok[mi]])
                        b.readers.append(ev)
                        evs.append(ev)
                    if half == 1 and ci is not None:
                        evs.append(vcopy(rw_[:, 0:3], STASH[:, ci * 4:ci * 4 + 3],
                                         deps=evs + [stash_toks.get(ci)]))
                    if ci is not None:
                        c0 = vs(cv_[:, 0:HALF], rw_[:, 0:HALF], CW[:, ci * 4:ci * 4 + 1],
                                ALU.mult, deps=evs + [cnv_tok[mi]] + cw_toks)
                        for i in range(1, 4):
                            c0 = stt(cv_[:, 0:HALF], rw_[:, i:i + HALF],
                                     CW[:, ci * 4 + i:ci * 4 + i + 1], cv_[:, 0:HALF],
                                     ALU.mult, ALU.add, deps=[c0])
                        raw_tok[mi] = c0
                        if half == 0:
                            stash_t = vcopy(STASH[:, ci * 4:ci * 4 + 3],
                                            rw_[:, 1024:1027], deps=evs + [c0])
                            stash_toks[ci] = stash_t
                            raw_tok[mi] = stash_t
                        sl = act(cv_[:, 0:HALF], cv_[:, 0:HALF], AF.Silu, deps=[c0])
                    else:
                        sl = act(cv_[:, 0:HALF], rw_[:, 3:3 + HALF], AF.Silu,
                                 deps=evs + [cnv_tok[mi]])
                        raw_tok[mi] = sl
                    post = sl
                    if m % 6 < 2:
                        sq2 = act(sqs[:, 0:HALF], cv_[:, 0:HALF], AF.Square,
                                  deps=[sl, sqs_g[0]])
                        ssr, rr2 = ROWS[0:1, 0:1024], ROWS[0:1, 1040:2064]
                        b5 = banks[0]
                        toks2 = []
                        for sj in (0, 512):
                            d5 = b5.wdeps()
                            tss = mmop(b5.ap[0:1, 0:512], ONEC, sqs[:, sj:sj + 512],
                                       deps=[sq2, t_onec] + d5)
                            b5.w = tss
                            sqs_g[0] = tss
                            ev5 = evac(ssr[0:1, sj:sj + 512], b5.ap[0:1, 0:512],
                                       deps=[tss])
                            b5.readers.append(ev5)
                            toks2.append(ev5)
                        tsq = act(ssr, ssr, AF.Sqrt, deps=toks2 + [e6],
                                  bias=EPSC[0:1, 0:1])
                        trc = vrecip(rr2, ssr, deps=[tsq])
                        b6 = banks[1]
                        for sj in (0, 512):
                            d6 = b6.wdeps()
                            tbc = mmop(b6.ap[:, 0:512], ONE1, rr2[0:1, sj:sj + 512],
                                       deps=[trc, t_one1] + d6)
                            b6.w = tbc
                            post = vt(cv_[:, sj:sj + 512], cv_[:, sj:sj + 512],
                                      b6.ap[:, 0:512], ALU.mult, deps=[tbc, post])
                            b6.readers.append(post)
                    dsp, idx = spill_dst(m)
                    sts = []
                    for c8 in range(8):
                        cg_ = half * 8 + c8
                        r0 = (idx * NCH + cg_) * 128
                        sts.append(st(dsp[r0:r0 + 128, :],
                                      cv_[:, c8 * 128:(c8 + 1) * 128], deps=[post]))
                    cnv_tok[mi] = sts[-1]

            bb = banks[6]
            for s0, w in ((0, 512), (512, 512)):
                dps0 = bb.wdeps()
                xs = (3 + s0) if half == 0 else s0
                for k in range(KT):
                    tmm = mm3op(bb.ap[0:64, 0:w],
                                WBAH[:, k * 64:(k + 1) * 64], WBAL[:, k * 64:(k + 1) * 64],
                                xnh(k, xs, w), xnl(k, xs, w),
                                deps=[t_wbal, xn_done[0], xn_done[1]] +
                                     (dps0 if k == 0 else []),
                                start=(k == 0), stop=(k == KT - 1))
                bb.w = tmm
                evb = evac(babuf[:, hb + s0:hb + s0 + w], bb.ap[0:64, 0:w],
                           deps=[tmm])
                bb.readers.append(evb)

        tb = act(babuf[0:32, :], babuf[0:32, :], AF.Sigmoid, deps=[evb])
        sp1 = ARNa[32:64, RAW_O:RAW_O + T]
        sp2 = ARNa[32:64, CNV_O:CNV_O + T]
        ty = act(babuf[32:64, :], babuf[32:64, :], AF.Identity, deps=[evb, t_sp,
                 raw_tok[0], raw_tok[1], cnv_tok[0], cnv_tok[1]],
                 bias=SMALLP[32:64, 1:2])
        tr_ = act(sp1, babuf[32:64, :], AF.Relu, deps=[ty])
        tab = act(sp2, babuf[32:64, :], AF.Abs, deps=[ty])
        te_ = act(sp2, sp2, AF.Exp, deps=[tab], scale=-1.0)
        tl_ = act(sp2, sp2, AF.Ln, deps=[te_], bias=ONEC[32:64, :])
        tg1 = vt(babuf[32:64, :], sp1, sp2, ALU.add, deps=[tl_, tr_])
        tg2 = vs(babuf[32:64, :], babuf[32:64, :], SMALLP[32:64, 2:3], ALU.mult,
                 deps=[tg1, t_nea])
        gbuf = ARNa[0:32, XST_O:XST_O + T]
        tgm = cast(gbuf, babuf[32:64, :], deps=[tg2, xf_tok[0], xf_tok[1],
                                                cnv_tok[0], cnv_tok[1]])
        cum_toks = []
        for c in range(NCH):
            tc_ = em.op("vector", lambda e, c=c: e.tensor_tensor_scan(
                cumh[:, c * C:(c + 1) * C], ONES32, gbuf[:, c * C:(c + 1) * C],
                0.0, op0=ALU.mult, op1=ALU.add), deps=[tgm, t_ones32])
            cum_toks.append(tc_)
        trt = []
        for c in range(NCH):
            b7 = banks[7]
            d7 = b7.wdeps()
            t1 = trop(b7.ap[:, 0:32], cumh[0:32, c * C:(c + 1) * C],
                      deps=[cum_toks[c], t_id] + d7, ident=IDC[0:32, 0:32])
            b7.w = t1
            ev = evac(cumt[:, c * 32:(c + 1) * 32], b7.ap[:, 0:32], deps=[t1])
            b7.readers.append(ev)
            d7 = b7.wdeps()
            t2 = trop(b7.ap[:, 0:32], babuf[0:32, c * C:(c + 1) * C],
                      deps=[tb, t_id] + d7, ident=IDC[0:32, 0:32])
            b7.w = t2
            ev2 = evac(btt[:, c * 32:(c + 1) * 32], b7.ap[:, 0:32], deps=[t2])
            b7.readers.append(ev2)
            trt.append(ev2)
        tnb = vs(nbtt, btt, NEG1, ALU.mult, deps=trt)
        for name in ("babuf", "cumh", "cumt", "btt"):
            if name in dbg_out:
                srcs = {"babuf": babuf, "cumh": cumh, "cumt": cumt, "btt": btt}
                st(dbg_out[name], srcs[name], deps=[tg2, tnb] + cum_toks)

    def passA():
        SL = {}
        names = ["kbuf", "qbuf", "vbuf", "ktok", "vtb", "Dm", "tel", "teu", "tei",
                 "NpA", "NpB", "TNp", "Wm", "bvt", "blk", "k2t", "u0s", "mks",
                 "mkts", "atile", "pct", "tmp32", "psT"]
        for i, n in enumerate(names):
            SL[n] = arn(i * 132, 128, 128)
        last = {n: None for n in names}
        ch_tok = {}
        prod_init = []
        sch_init = vmemset(Schain, 0.0)
        for vh in range(HV):
            prod_init.append(vcopy(PROD[:, vh * 128:(vh + 1) * 128], IDC, deps=[t_id]))
        schain_tok = [sch_init] * HV
        prod_tok = list(prod_init)

        bG, bW1, bW2, bW3, bW4 = banks[0], banks[1], banks[2], banks[3], banks[4]
        bG2 = banks[5]

        ch_readers = []
        for c in range(NCH):
            csl = slice(c * 32, (c + 1) * 32)
            t_expc = act(MISC[:, 0:32], cumt[:, csl], AF.Exp,
                         deps=list(ch_readers))
            tm32 = vs(SL["tmp32"][0:32, 0:128], ONESF[0:32, :],
                      cumh[:, c * C + 127:c * C + 128], ALU.mult,
                      deps=[t_onesf, last["tmp32"]])
            d4 = bW4.wdeps()
            tlog = trop(bW4.ap[:, 0:32], SL["tmp32"][0:32, 0:128],
                        deps=[tm32, t_id] + d4, ident=IDC[0:32, 0:32])
            bW4.w = tlog
            te2l = vt(MISC[:, 32:64], bW4.ap[:, 0:32], cumt[:, csl], ALU.subtract,
                      deps=[tlog] + ch_readers)
            te2 = act(MISC[:, 32:64], MISC[:, 32:64], AF.Exp, deps=[te2l])
            tlc = act(MISC[:, 64:96], bW4.ap[:, 0:32], AF.Exp, deps=[tlog, te2l])
            tlcs = vcopy(LAMC[:, csl], MISC[:, 64:96], deps=[tlc])
            bW4.readers += [te2l, tlc]
            last["tmp32"] = te2l
            ch = {"expc": t_expc, "e2m": te2, "lamC": tlc}
            ch_readers = [tlcs]

            for h in range(HK):
                r0 = (h * NCH + c) * 128
                lk = ld(SL["kbuf"], ksp[r0:r0 + 128, :], deps=[last["kbuf"]])
                lq = ld(SL["qbuf"], qsp[r0:r0 + 128, :], deps=[last["qbuf"]])
                dG = bG.wdeps()
                tg1_ = mmop(bG.ap[:, 0:128], SL["kbuf"], SL["kbuf"], deps=[lk] + dG)
                bG.w = tg1_
                dG2 = bG2.wdeps()
                tg2_ = mmop(bG2.ap[:, 0:128], SL["kbuf"], SL["qbuf"], deps=[lk, lq] + dG2)
                bG2.w = tg2_
                dW = bW1.wdeps()
                tkt = trop(bW1.ap[:, 0:128], SL["kbuf"], deps=[lk, t_id] + dW)
                bW1.w = tkt
                ektok = evac(SL["ktok"], bW1.ap[:, 0:128], deps=[tkt, last["ktok"]])
                bW1.readers.append(ektok)
                last["kbuf"], last["qbuf"] = tg2_, tg2_
                last["ktok"] = ektok

                for vh in (2 * h, 2 * h + 1):
                    vr0 = (vh * NCH + c) * 128
                    ccol = cumt[:, c * 32 + vh:c * 32 + vh + 1]
                    bcol = btt[:, c * 32 + vh:c * 32 + vh + 1]
                    nbcol = nbtt[:, c * 32 + vh:c * 32 + vh + 1]
                    lcol = MISC[:, 0 + vh:1 + vh]
                    e2col = MISC[:, 32 + vh:33 + vh]
                    lCcol = MISC[:, 64 + vh:65 + vh]
                    lv = ld(SL["vbuf"], vsp[vr0:vr0 + 128, :], deps=[last["vbuf"]])
                    dW = bW2.wdeps()
                    tvt = trop(bW2.ap[:, 0:128], SL["vbuf"], deps=[lv, t_id] + dW)
                    bW2.w = tvt
                    evt = evac(SL["vtb"], bW2.ap[:, 0:128], deps=[tvt, last["vtb"]])
                    bW2.readers.append(evt)
                    last["vbuf"] = tvt
                    tcc = vs(SL["Dm"], ONESF, ccol, ALU.mult,
                             deps=[t_onesf, last["Dm"]])
                    dW = bW3.wdeps()
                    tcb = trop(bW3.ap[:, 0:128], SL["Dm"], deps=[tcc, t_id] + dW)
                    bW3.w = tcb
                    tdm = vt(SL["Dm"], SL["Dm"], bW3.ap[:, 0:128], ALU.subtract,
                             deps=[tcb])
                    bW3.readers.append(tdm)
                    t_el = vt(SL["tel"], SL["Dm"], MSK[:, 0:128], ALU.add,
                              deps=[tdm, t_msk, last["tel"]])
                    t_el = act(SL["tel"], SL["tel"], AF.Exp, deps=[t_el])
                    t_eu = stt(SL["teu"], SL["Dm"], NEG1[:, 0:1], MSK[:, 128:256],
                               ALU.mult, ALU.add, deps=[tdm, last["teu"], t_neg1])
                    t_eu = act(SL["teu"], SL["teu"], AF.Exp, deps=[t_eu])
                    t_ei = stt(SL["tei"], SL["Dm"], NEG1[:, 0:1], MSK[:, 256:384],
                               ALU.mult, ALU.add, deps=[tdm, last["tei"]])
                    t_ei = act(SL["tei"], SL["tei"], AF.Exp, deps=[t_ei])
                    last["Dm"] = t_ei
                    tnp = vt(SL["NpA"], SL["tel"], bG.ap[:, 0:128], ALU.mult,
                             deps=[t_el, bG.w, last["NpA"]])
                    tnp = vs(SL["NpA"], SL["NpA"], nbcol, ALU.mult,
                             deps=[tnp])
                    last["tel"] = tnp
                    tnbb = vs(SL["tmp32"], ONESF, nbcol, ALU.mult,
                              deps=[t_onesf, last["tmp32"]])
                    dW = bW3.wdeps()
                    tnbt = trop(bW3.ap[:, 0:128], SL["tmp32"], deps=[tnbb, t_id] + dW)
                    bW3.w = tnbt
                    last["tmp32"] = tnbb
                    tb_ = vt(SL["teu"], SL["teu"], bG.ap[:, 0:128], ALU.mult,
                             deps=[t_eu, bG.w])
                    tb_ = vt(SL["teu"], SL["teu"], bW3.ap[:, 0:128], ALU.mult,
                             deps=[tb_, tnbt])
                    bW3.readers.append(tb_)
                    bG.readers += [tnp, tb_]
                    tat = vt(SL["atile"], SL["tei"], bG2.ap[:, 0:128], ALU.mult,
                             deps=[t_ei, bG2.w, last["atile"]])
                    bG2.readers.append(tat)
                    last["tei"] = tat
                    sat = st(atsp[vr0:vr0 + 128, :], SL["atile"], deps=[tat])
                    last["atile"] = sat
                    twm = vt(SL["Wm"], SL["teu"], IDC, ALU.add,
                             deps=[tb_, t_id, last["Wm"]])
                    last["teu"] = twm
                    np_cur = "NpA"
                    np_tok = tnp
                    for j in range(1, 7):
                        dW = bW1.wdeps()
                        t_tr = trop(bW1.ap[:, 0:128], SL[np_cur], deps=[np_tok, t_id] + dW)
                        bW1.w = t_tr
                        e_tr = evac(SL["TNp"], bW1.ap[:, 0:128],
                                    deps=[t_tr, last["TNp"]])
                        bW1.readers.append(e_tr)
                        np_nxt = "NpB" if np_cur == "NpA" else "NpA"
                        dW = bW2.wdeps()
                        t_sq = mmop(bW2.ap[:, 0:128], SL["TNp"], SL[np_cur],
                                    deps=[e_tr, np_tok] + dW)
                        bW2.w = t_sq
                        e_sq = evac(SL[np_nxt], bW2.ap[:, 0:128],
                                    deps=[t_sq, last[np_nxt]])
                        bW2.readers.append(e_sq)
                        last["TNp"] = t_sq
                        dW = bW3.wdeps()
                        t_ch = mmop(bW3.ap[:, 0:128], SL[np_nxt], SL["Wm"],
                                    deps=[e_sq, twm] + dW)
                        bW3.w = t_ch
                        twm = vt(SL["Wm"], SL["Wm"], bW3.ap[:, 0:128], ALU.add,
                                 deps=[t_ch])
                        bW3.readers.append(twm)
                        last[np_cur] = t_sq
                        last[np_nxt] = t_ch
                        np_cur, np_tok = np_nxt, e_sq
                    last["Wm"] = twm
                    tbv = vs(SL["bvt"], SL["vtb"], bcol, ALU.mult,
                             deps=[evt, last["bvt"]])
                    dW = bW1.wdeps()
                    tu0 = mmop(bW1.ap[:, 0:128], SL["Wm"], SL["bvt"],
                               deps=[twm, tbv] + dW)
                    bW1.w = tu0
                    eu0 = evac(SL["u0s"], bW1.ap[:, 0:128], deps=[tu0, last["u0s"]])
                    bW1.readers.append(eu0)
                    su0 = st(u0sp[vr0:vr0 + 128, :], SL["u0s"], deps=[eu0])
                    last["bvt"] = tu0
                    tbl = vt(MISC[:, 128:129], bcol, lcol, ALU.mult,
                             deps=[ch["expc"], last.get("blcol")])
                    last["blcol"] = tbl
                    tblk = vs(SL["blk"], SL["ktok"], MISC[:, 128:129], ALU.mult,
                              deps=[ektok, tbl, last["blk"]])
                    last["blcol"] = tblk
                    dW = bW2.wdeps()
                    tmk = mmop(bW2.ap[:, 0:128], SL["Wm"], SL["blk"],
                               deps=[twm, tblk] + dW)
                    bW2.w = tmk
                    emk = evac(SL["mks"], bW2.ap[:, 0:128], deps=[tmk, last["mks"]])
                    bW2.readers.append(emk)
                    last["blk"] = tmk
                    dW = bW3.wdeps()
                    tmkt = trop(bW3.ap[:, 0:128], SL["mks"], deps=[emk, t_id] + dW)
                    bW3.w = tmkt
                    emkt = evac(SL["mkts"], bW3.ap[:, 0:128], deps=[tmkt, last["mkts"]])
                    bW3.readers.append(emkt)
                    smkt = st(mktsp[vr0:vr0 + 128, :], SL["mkts"], deps=[emkt])
                    last["mkts"] = smkt
                    tk2 = vs(SL["k2t"], SL["ktok"], e2col, ALU.mult,
                             deps=[ektok, ch["e2m"], last["k2t"]])
                    sk2 = st(k2tsp[vr0:vr0 + 128, :], SL["k2t"], deps=[tk2])
                    dW = bW1.wdeps()
                    tqc = mmop(bW1.ap[:, 0:128], SL["k2t"], SL["u0s"],
                               deps=[tk2, eu0] + dW, start=True, stop=False)
                    bW1.w = tqc
                    last["u0s"] = tqc
                    dW = bW2.wdeps()
                    tkp = mmop(bW2.ap[:, 0:128], SL["k2t"], SL["mks"],
                               deps=[tk2, emk] + dW)
                    bW2.w = tkp
                    last["mks"] = tkp
                    last["k2t"] = tkp
                    ekp = evac(SL["TNp"], bW2.ap[:, 0:128], deps=[tkp, last["TNp"]])
                    bW2.readers.append(ekp)
                    last["TNp"] = ekp
                    dW = bW3.wdeps()
                    tkpt = trop(bW3.ap[:, 0:128], SL["TNp"], deps=[ekp, t_id] + dW)
                    bW3.w = tkpt
                    tpct = stt(SL["pct"], IDC, lCcol, bW3.ap[:, 0:128],
                               ALU.mult, ALU.subtract,
                               deps=[tkpt, ch["lamC"], t_id, last["pct"]])
                    bW3.readers.append(tpct)
                    ssl = Schain[:, vh * 128:(vh + 1) * 128]
                    tsc = mmop(bW1.ap[:, 0:128], SL["pct"], ssl,
                               deps=[tpct, schain_tok[vh], tqc],
                               start=False, stop=True)
                    bW1.w = tsc
                    tsn = vcopy(ssl, bW1.ap[:, 0:128], deps=[tsc])
                    bW1.readers.append(tsn)
                    schain_tok[vh] = tsn
                    psl = PROD[:, vh * 128:(vh + 1) * 128]
                    dW = bW4.wdeps()
                    tpp = mmop(bW4.ap[:, 0:128], SL["pct"], psl,
                               deps=[tpct, prod_tok[vh]] + dW)
                    bW4.w = tpp
                    epp = evac(psl, bW4.ap[:, 0:128], deps=[tpp])
                    bW4.readers.append(epp)
                    prod_tok[vh] = epp
                    last["pct"] = tpp
                    last["vtb"] = tbv
                    ch_readers += [tbl, tk2, tpct]

        cc_toks = []
        for vh in range(HV):
            psl = PROD[:, vh * 128:(vh + 1) * 128]
            dW = bW1.wdeps()
            tps = trop(bW1.ap[:, 0:128], psl, deps=[prod_tok[vh], t_id] + dW)
            bW1.w = tps
            eps_ = evac(SL["psT"], bW1.ap[:, 0:128], deps=[tps, last["psT"]])
            bW1.readers.append(eps_)
            t1 = st(ccin[(vh * 2) * 128:(vh * 2) * 128 + 128, :], SL["psT"],
                    deps=[eps_])
            last["psT"] = t1
            t2 = st(ccin[(vh * 2 + 1) * 128:(vh * 2 + 1) * 128 + 128, :],
                    Schain[:, vh * 128:(vh + 1) * 128], deps=[schain_tok[vh]])
            cc_toks += [t1, t2]
        if "ccin" in dbg_out:
            st(dbg_out["ccin"], ccin, deps=cc_toks)
        tcc = em.op("gpsimd", lambda e: e.collective_compute(
            "AllGather", ALU.bypass, replica_groups=[list(range(NC_))],
            ins=[ccin.opt() if hasattr(ccin, "opt") else ccin],
            outs=[ccout.opt() if hasattr(ccout, "opt") else ccout]),
            deps=cc_toks, inc=1)
        if "ccout" in dbg_out:
            st(dbg_out["ccout"], ccout, deps=[tcc])
        pldA = pldB = None
        s0_toks = []
        for vh in range(HV):
            s0_toks.append(vmemset(Schain[:, vh * 128:(vh + 1) * 128], 0.0,
                                   deps=[schain_tok[vh], cc_toks[-1]]))
        sst_tok = s0_toks
        isub_tok = None
        for j in range(NC_ - 1):
            mcol = CMASK[:, j:j + 1]
            ti1 = vs(SL["tmp32"], IDC, mcol, ALU.mult,
                     deps=[t_id, t_cm, last["tmp32"]])
            ti2 = vt(SL["TNp"], IDC, SL["tmp32"], ALU.subtract,
                     deps=[ti1, last["TNp"]])
            last["tmp32"], last["TNp"] = ti2, ti2
            for vh in range(HV):
                base = (j * HV * 2 + vh * 2) * 128
                l1 = ld(SL["kbuf"], ccout[base:base + 128, :],
                        deps=[tcc, last["kbuf"]])
                l2 = ld(SL["qbuf"], ccout[base + 128:base + 256, :],
                        deps=[tcc, last["qbuf"]])
                tpm = vs(SL["pct"], SL["kbuf"], mcol, ALU.mult,
                         deps=[l1, last["pct"]])
                tpm = vt(SL["pct"], SL["pct"], SL["TNp"], ALU.add,
                         deps=[tpm, ti2])
                tqm = vs(SL["blk"], SL["qbuf"], mcol, ALU.mult,
                         deps=[l2, last["blk"]])
                ssl = Schain[:, vh * 128:(vh + 1) * 128]
                dW = bW1.wdeps()
                tsm = mmop(bW1.ap[:, 0:128], SL["pct"], ssl,
                           deps=[tpm, sst_tok[vh]] + dW)
                bW1.w = tsm
                tsn = vt(ssl, bW1.ap[:, 0:128], SL["blk"], ALU.add,
                         deps=[tsm, tqm])
                bW1.readers.append(tsn)
                sst_tok[vh] = tsn
                last["kbuf"], last["qbuf"] = tsm, tsm
                last["pct"], last["blk"] = tsm, tsn
                last["TNp"] = tpm
        if "sst" in dbg_out:
            st(dbg_out["sst"], Schain, deps=sst_tok)
        return sst_tok

    def passB(sst_tok):
        names = ["u0", "mkt", "at", "k2t", "qc", "zc", "U", "O", "cgf", "hi", "dum"]
        SLs, hibs, lobs, lasts = [], [], [], []
        for pp in range(2):
            base = 4000 + pp * (len(names) * 132 + 140)
            SLs.append({n: arn(base + i * 132, 128, 128)
                        for i, n in enumerate(names)})
            hibs.append(arnb(base + len(names) * 132, 128))
            lobs.append(arnb(base + len(names) * 132 + 64, 128))
            lasts.append({n: None for n in names})
            lasts[pp]["hib"] = None
            lasts[pp]["lob"] = None
        for vh in range(HV):
            ssl = Schain[:, vh * 128:(vh + 1) * 128]
            stok = sst_tok[vh]
            for c in range(NCH):
                p_ = (vh * NCH + c) & 1
                SL, last = SLs[p_], lasts[p_]
                hib, lob = hibs[p_], lobs[p_]
                b1, b2, b3, b4 = (banks[0], banks[1], banks[2], banks[3]) if p_ == 0 \
                    else (banks[4], banks[5], banks[6], banks[7])
                vr0 = (vh * NCH + c) * 128
                qr0 = ((vh // 2) * NCH + c) * 128
                ccol = cumt[:, c * 32 + vh:c * 32 + vh + 1]
                l_u0 = ld(SL["u0"], u0sp[vr0:vr0 + 128, :], deps=[last["u0"]])
                l_mk = ld(SL["mkt"], mktsp[vr0:vr0 + 128, :], deps=[last["mkt"]])
                l_at = ld(SL["at"], atsp[vr0:vr0 + 128, :], deps=[last["at"]])
                l_k2 = ld(SL["k2t"], k2tsp[vr0:vr0 + 128, :], deps=[last["k2t"]])
                l_q = ld(SL["qc"], qsp[qr0:qr0 + 128, :], deps=[last["qc"]])
                l_z = ld(SL["zc"], zsp[vr0:vr0 + 128, :], deps=[last["zc"]])
                mb = 96 + p_ * 8
                texp = act(MISC[:, mb:mb + 1], ccol, AF.Exp, deps=[last.get("expc")])
                last["expc"] = texp
                lamCc = LAMC[:, c * 32 + vh:c * 32 + vh + 1]
                tl2 = None
                d1 = b1.wdeps()
                tmu = mmop(b1.ap[:, 0:128], SL["mkt"], ssl, deps=[l_mk, stok] + d1)
                b1.w = tmu
                tU = vt(SL["U"], SL["u0"], b1.ap[:, 0:128], ALU.subtract,
                        deps=[tmu, l_u0, last["U"]])
                b1.readers.append(tU)
                last["u0"] = tU
                last["mkt"] = tmu
                d2 = b2.wdeps()
                toq = mmop(b2.ap[:, 0:128], SL["qc"], ssl, deps=[l_q, stok] + d2)
                b2.w = toq
                d3 = b3.wdeps()
                tau = mmop(b3.ap[:, 0:128], SL["at"], SL["U"], deps=[l_at, tU] + d3)
                b3.w = tau
                eoq = evac(SL["hi"], b2.ap[:, 0:128], deps=[toq, last["hi"]])
                b2.readers.append(eoq)
                last["hi"] = eoq
                tO = stt(SL["O"], SL["hi"], MISC[:, mb:mb + 1], b3.ap[:, 0:128],
                         ALU.mult, ALU.add, deps=[eoq, tau, texp, last["O"]])
                last["expc"] = tO
                b3.readers.append(tO)
                last["at"] = tau
                last["qc"] = toq
                d1 = b1.wdeps()
                tsu = mmop(b1.ap[:, 0:128], SL["k2t"], SL["U"], deps=[l_k2, tU] + d1)
                b1.w = tsu
                tsn = stt(ssl, ssl, lamCc, b1.ap[:, 0:128], ALU.mult, ALU.add,
                          deps=[tsu, tl2, stok])
                b1.readers.append(tsn)
                stok = tsn
                last["k2t"] = tsu
                last["U"] = tsu
                tms = act(SL["dum"], SL["O"], AF.Square, deps=[tO, last["dum"]],
                          accum=MISC[:, mb + 2:mb + 3])
                trs = act(MISC[:, mb + 2:mb + 3], MISC[:, mb + 2:mb + 3], AF.Sqrt,
                          deps=[tms, e6], scale=1.0 / 128, bias=EPSC[:, 0:1])
                trc = vrecip(MISC[:, mb + 3:mb + 4], MISC[:, mb + 2:mb + 3], deps=[trs])
                tog = vs(SL["O"], SL["O"], MISC[:, mb + 3:mb + 4], ALU.mult,
                         deps=[tO, trc, tms])
                last["dum"] = tms
                d4 = b4.wdeps()
                tot = trop(b4.ap[:, 0:128], SL["O"], deps=[tog, t_id] + d4)
                b4.w = tot
                tcg = stt(SL["cgf"], b4.ap[:, 0:128], NWC[:, 0:1], SL["zc"],
                          ALU.mult, ALU.mult, deps=[tot, t_nw, l_z, last["cgf"]])
                b4.readers.append(tcg)
                last["O"] = tot
                last["zc"] = tcg
                th_ = cast(hib, SL["cgf"], deps=[tcg, last["hib"]])
                tl_ = vt(lob, SL["cgf"], hib, ALU.subtract, deps=[th_, last["lob"]])
                last["cgf"] = tl_
                s1 = st(cghsp[vh * 128:(vh + 1) * 128, c * 128:(c + 1) * 128],
                        hib, deps=[tl_])
                s2 = st(cglsp[vh * 128:(vh + 1) * 128, c * 128:(c + 1) * 128],
                        lob, deps=[tl_])
                last["hib"], last["lob"] = s1, s2

    def phase3():
        WFO = 0
        CGO = 2 * 132 * 8 + 600
        wfr = [arn(WFO + i * 260, 128, 256) for i in range(2)]
        wh_ = [arnb(WFO + 2 * 260 + i * 130, 256) for i in range(2)]
        wl_ = [arnb(WFO + 2 * 260 + 2 * 130 + i * 130, 256) for i in range(2)]
        cgh = [arnb(CGO + i * 1040, 2048) for i in range(2)]
        cgl = [arnb(CGO + 2 * 1040 + i * 1040, 2048) for i in range(2)]
        att = [arn(CGO + 4 * 1040 + 32 + i * 2064, 128, 2048) for i in range(2)]
        wtok = [None, None]
        whtok = [None, None]
        cgtok = [None, None]
        attok = [None, None]
        for mp in range(8):
            bq = banks[0:8]
            bdep = [b.wdeps() for b in bq]
            for k in range(32):
                ks = k % 2
                lw = ld(wfr[ks], woutt[k * 128:(k + 1) * 128, mp * 256:(mp + 1) * 256],
                        deps=[wtok[ks]])
                hw = cast(wh_[ks], wfr[ks], deps=[lw, whtok[ks]])
                lw2 = vt(wl_[ks], wfr[ks], wh_[ks], ALU.subtract, deps=[hw])
                wtok[ks] = lw2
                lh_ = ld(cgh[ks], cghsp[k * 128:(k + 1) * 128, :], deps=[cgtok[ks]])
                ll_ = ld(cgl[ks], cglsp[k * 128:(k + 1) * 128, :], deps=[lh_])
                for mi in range(2):
                    for si in range(4):
                        b = bq[mi * 4 + si]
                        dps = [lw2, ll_]
                        if k == 0:
                            dps += bdep[mi * 4 + si]
                        tmm = mm3op(b.ap[:, 0:512],
                                    wh_[ks][:, mi * 128:(mi + 1) * 128],
                                    wl_[ks][:, mi * 128:(mi + 1) * 128],
                                    cgh[ks][:, si * 512:(si + 1) * 512],
                                    cgl[ks][:, si * 512:(si + 1) * 512],
                                    deps=dps, start=(k == 0), stop=(k == 31))
                        if k == 31:
                            b.w = tmm
                cgtok[ks] = tmm
                whtok[ks] = tmm
            for mi in range(2):
                m = mp * 2 + mi
                evs = []
                for si in range(4):
                    b = bq[mi * 4 + si]
                    ev = evac(att[mi][:, si * 512:(si + 1) * 512], b.ap[:, 0:512],
                              deps=[b.w, attok[mi]])
                    b.readers.append(ev)
                    evs.append(ev)
                t = st(attnsp[m * 128:(m + 1) * 128, :], att[mi], deps=evs)
                attok[mi] = t
        if "attnsp" in dbg_out:
            st(dbg_out["attnsp"], attnsp, deps=[attok[0], attok[1]])
        em.barrier()
        ln_residual(attnsp, xt, x2sp, xoff=3, dbgname="x2sp")

    def ln_residual(src_sp, res_sp, dst_sp, xoff, dbgname=None, to_out=False):
        ATO = 0
        atile = [arn(ATO + i * 520, 128, 512) for i in range(16)]
        sq_ = arn(ATO + 16 * 520 + 16, 128, 512)
        xre = [arn(ATO + 16 * 520 + 544 + i * 520, 128, 512) for i in range(2)]
        at_tok = [None] * 16
        xr_tok = [None, None]
        sqg = None
        dtoks = []
        for s0 in range(0, T, 512):
            b0, b1 = banks[0], banks[1]
            d0, d1 = b0.wdeps(), b1.wdeps()
            for k in range(KT):
                lt = ld(atile[k], src_sp[k * 128:(k + 1) * 128, s0:s0 + 512],
                        deps=[at_tok[k]])
                tq = act(sq_, atile[k], AF.Square, deps=[lt, sqg])
                t1 = mmop(b0.ap[0:1, 0:512], ONEC, atile[k],
                          deps=[lt, t_onec] + (d0 if k == 0 else []),
                          start=(k == 0), stop=(k == KT - 1))
                t2 = mmop(b1.ap[0:1, 0:512], ONEC, sq_,
                          deps=[tq] + (d1 if k == 0 else []),
                          start=(k == 0), stop=(k == KT - 1))
                at_tok[k] = t1
                sqg = t2
            b0.w, b1.w = t1, t2
            ev0 = evac(ROWS[0:1, 0:512], b0.ap[0:1, 0:512], deps=[t1])
            ev1 = evac(ROWS[0:1, 1040:1552], b1.ap[0:1, 0:512], deps=[t2])
            b0.readers.append(ev0)
            b1.readers.append(ev1)
            tmu = vss(ROWS[0:1, 0:512], ROWS[0:1, 0:512], 1.0 / D, ALU.mult,
                      deps=[ev0])
            tv1 = vss(ROWS[0:1, 1040:1552], ROWS[0:1, 1040:1552], 1.0 / D,
                      ALU.mult, deps=[ev1])
            tm2 = vt(ROWS[0:1, 2080:2592], ROWS[0:1, 0:512], ROWS[0:1, 0:512],
                     ALU.mult, deps=[tmu])
            tvr = vt(ROWS[0:1, 1040:1552], ROWS[0:1, 1040:1552],
                     ROWS[0:1, 2080:2592], ALU.subtract, deps=[tv1, tm2])
            tsd = act(ROWS[0:1, 1040:1552], ROWS[0:1, 1040:1552], AF.Sqrt,
                      deps=[tvr, e5], bias=EPSC[0:1, 1:2])
            trr = vrecip(ROWS[0:1, 3120:3632], ROWS[0:1, 1040:1552],
                         deps=[tsd])
            tmr = vt(ROWS[0:1, 2080:2592], ROWS[0:1, 0:512],
                     ROWS[0:1, 3120:3632], ALU.mult, deps=[trr, tm2])
            b2, b3 = banks[2], banks[3]
            rb_t = mmop(b2.ap[:, 0:512], ONE1, ROWS[0:1, 3120:3632],
                        deps=[trr, t_one1] + b2.wdeps())
            mr_t = mmop(b3.ap[:, 0:512], ONE1, ROWS[0:1, 2080:2592],
                        deps=[tmr, t_one1] + b3.wdeps())
            b2.w, b3.w = rb_t, mr_t
            for k in range(KT):
                ks = k % 2
                if res_sp is xt:
                    lr = ld(xre[ks], res_sp[k * 128:(k + 1) * 128,
                                           xoff + s0:xoff + s0 + 512],
                            deps=[xr_tok[ks]])
                else:
                    lr = ld(xre[ks], res_sp[k * 128:(k + 1) * 128, s0:s0 + 512],
                            deps=[xr_tok[ks]])
                f1 = vt(atile[k], atile[k], b2.ap[:, 0:512], ALU.mult,
                        deps=[at_tok[k], rb_t])
                f2 = vt(atile[k], atile[k], b3.ap[:, 0:512], ALU.subtract,
                        deps=[f1, mr_t])
                f3 = vt(atile[k], atile[k], xre[ks], ALU.add, deps=[f2, lr])
                b2.readers.append(f1)
                b3.readers.append(f2)
                xr_tok[ks] = f3
                tstore = st(dst_sp[k * 128:(k + 1) * 128, s0:s0 + 512], atile[k],
                            deps=[f3])
                at_tok[k] = tstore
                dtoks.append(tstore)
        if dbgname and dbgname in dbg_out:
            st(dbg_out[dbgname], dst_sp, deps=dtoks)
        return dtoks

    def phase4():
        def xn2h(k):
            return ARNb[:, k * 512:k * 512 + 512]

        def xn2l(k):
            return ARNb[:, 8192 + k * 512:8192 + k * 512 + 512]

        P4P = 8192
        ASTH = [arnb(P4P + k * 128, 256) for k in range(KT)]
        ASTL = [arnb(P4P + 2048 + k * 128, 256) for k in range(KT)]
        BSTH = [arnb(P4P + 4096 + kb * 1024, 2048) for kb in range(2)]
        BSTL = [arnb(P4P + 6144 + kb * 1024, 2048) for kb in range(2)]
        RW1H = [arnb(P4P + 8192 + k * 32, 64) for k in range(KT)]
        RW1L = [arnb(P4P + 8704 + k * 32, 64) for k in range(KT)]
        WFFH = [arnb(P4P + 9216 + k * 128, 256) for k in range(KT)]
        WFFL = [arnb(P4P + 11264 + k * 128, 256) for k in range(KT)]
        R1 = P4P + 13312
        hsb = arn(R1, 64, 512)
        hsbh = arnb(R1 + 512, 512, p=64)
        hsbl = arnb(R1 + 768, 512, p=64)
        lgb = arn(R1 + 1024, 16, 512)
        coeff = arn(R1 + 1536, 16, 512)
        xab = [arn(R1 + 2048 + e * 512, 128, 512) for e in range(2)]
        WXH = [arnb(R1 + 3072 + e * 256, 512) for e in range(2)]
        WXL = [arnb(R1 + 3584 + e * 256, 512) for e in range(2)]
        LT = arn(R1 + 4096, 128, 64)
        x2r = [arn(R1 + 4160 + i * 512, 128, 512) for i in range(2)]
        sqr = arn(R1 + 5184, 128, 512)
        wstg = arn(R1 + 5696, 128, 256)
        fstg = [arn(R1 + 5952 + i * 512, 128, 512) for i in range(2)]

        wst_tok = None
        wg = [None]
        stat_last = None
        for k in range(KT):
            lw = ld(wstg, asttd[k * 128:(k + 1) * 128, :], deps=[wst_tok])
            hc = cast(ASTH[k], wstg, deps=[lw])
            lc = vt(ASTL[k], wstg, ASTH[k], ALU.subtract, deps=[hc])
            wst_tok = lc
        for kb in range(2):
            for hq in range(8):
                lw = ld(wstg, bsttd[kb * 128:(kb + 1) * 128, hq * 256:(hq + 1) * 256],
                        deps=[wst_tok])
                hc = cast(BSTH[kb][:, hq * 256:(hq + 1) * 256], wstg[:, 0:256],
                          deps=[lw])
                lc = vt(BSTL[kb][:, hq * 256:(hq + 1) * 256], wstg[:, 0:256],
                        BSTH[kb][:, hq * 256:(hq + 1) * 256], ALU.subtract, deps=[hc])
                wst_tok = lc
        for k in range(KT):
            lw = ld(wstg[:, 0:64], rw1td[k * 128:(k + 1) * 128, :], deps=[wst_tok])
            hc = cast(RW1H[k], wstg[:, 0:64], deps=[lw])
            lc = vt(RW1L[k], wstg[:, 0:64], RW1H[k], ALU.subtract, deps=[hc])
            wst_tok = lc
        stat_last = wst_tok

        x2_tok = [None, None]
        sqrg = [None]
        wffg = [None]
        fst_tok = [None, None]
        lt_tok = None
        cf_tok = None
        xn2_g = [stat_last]
        for q in range(4):
            s0 = q * 512
            b0, b1 = banks[0], banks[1]
            d0, d1 = b0.wdeps(), b1.wdeps()
            for k in range(KT):
                ks = k % 2
                lt = ld(x2r[ks], x2sp[k * 128:(k + 1) * 128, s0:s0 + 512],
                        deps=[x2_tok[ks]])
                tq = act(sqr, x2r[ks], AF.Square, deps=[lt, sqrg[0]])
                t1 = mmop(b0.ap[0:1, 0:512], ONEC, x2r[ks],
                          deps=[lt, t_onec] + (d0 if k == 0 else []),
                          start=(k == 0), stop=(k == KT - 1))
                t2 = mmop(b1.ap[0:1, 0:512], ONEC, sqr,
                          deps=[tq] + (d1 if k == 0 else []),
                          start=(k == 0), stop=(k == KT - 1))
                x2_tok[ks] = t1
                sqrg[0] = t2
            b0.w, b1.w = t1, t2
            ev0 = evac(ROWS[0:1, 0:512], b0.ap[0:1, 0:512], deps=[t1])
            ev1 = evac(ROWS[0:1, 1040:1552], b1.ap[0:1, 0:512], deps=[t2])
            b0.readers.append(ev0)
            b1.readers.append(ev1)
            tmu = vss(ROWS[0:1, 0:512], ROWS[0:1, 0:512], 1.0 / D, ALU.mult,
                      deps=[ev0])
            tv1 = vss(ROWS[0:1, 1040:1552], ROWS[0:1, 1040:1552], 1.0 / D,
                      ALU.mult, deps=[ev1])
            tm2 = vt(ROWS[0:1, 2080:2592], ROWS[0:1, 0:512], ROWS[0:1, 0:512],
                     ALU.mult, deps=[tmu])
            tvr = vt(ROWS[0:1, 1040:1552], ROWS[0:1, 1040:1552],
                     ROWS[0:1, 2080:2592], ALU.subtract, deps=[tv1, tm2])
            tsd = act(ROWS[0:1, 1040:1552], ROWS[0:1, 1040:1552], AF.Sqrt,
                      deps=[tvr, e5], bias=EPSC[0:1, 1:2])
            trr = vrecip(ROWS[0:1, 3120:3632], ROWS[0:1, 1040:1552], deps=[tsd])
            tmr = vt(ROWS[0:1, 2080:2592], ROWS[0:1, 0:512],
                     ROWS[0:1, 3120:3632], ALU.mult, deps=[trr, tm2])
            b2, b3 = banks[2], banks[3]
            rb_t = mmop(b2.ap[:, 0:512], ONE1, ROWS[0:1, 3120:3632],
                        deps=[trr, t_one1] + b2.wdeps())
            mr_t = mmop(b3.ap[:, 0:512], ONE1, ROWS[0:1, 2080:2592],
                        deps=[tmr, t_one1] + b3.wdeps())
            b2.w, b3.w = rb_t, mr_t
            xn2_tok = [None] * KT
            for k in range(KT):
                ks = k % 2
                lt = ld(x2r[ks], x2sp[k * 128:(k + 1) * 128, s0:s0 + 512],
                        deps=[x2_tok[ks]])
                f1 = vt(sqr, x2r[ks], b2.ap[:, 0:512], ALU.mult,
                        deps=[lt, rb_t, sqrg[0], xn2_g[0]])
                f2 = vt(sqr, sqr, b3.ap[:, 0:512], ALU.subtract, deps=[f1, mr_t])
                b2.readers.append(f1)
                b3.readers.append(f2)
                hc = cast(xn2h(k), sqr, deps=[f2])
                lc = vt(xn2l(k), sqr, xn2h(k), ALU.subtract, deps=[hc])
                x2_tok[ks] = lc
                sqrg[0] = lc
                xn2_tok[k] = lc
            xn2_last = xn2_tok[KT - 1]
            b4 = banks[4]
            d4 = b4.wdeps()
            for k in range(KT):
                tmm = mm3op(b4.ap[0:64, 0:512], RW1H[k], RW1L[k],
                            xn2h(k), xn2l(k),
                            deps=[stat_last, xn2_tok[k]] + (d4 if k == 0 else []),
                            start=(k == 0), stop=(k == KT - 1))
            b4.w = tmm
            th_ = act(hsb, b4.ap[0:64, 0:512], AF.Silu,
                      deps=[tmm, t_rb1], bias=RB1)
            b4.readers.append(th_)
            thh = cast(hsbh, hsb, deps=[th_])
            thl = vt(hsbl, hsb, hsbh, ALU.subtract, deps=[thh])
            d4 = b4.wdeps()
            tmm = mm3op(b4.ap[0:16, 0:512], RW2H, RW2L, hsbh, hsbl,
                        deps=[thl, t_rw2l] + d4, start=True, stop=True)
            b4.w = tmm
            tlg = act(lgb, b4.ap[0:16, 0:512], AF.Identity,
                      deps=[tmm, t_rbg], bias=RB2G[:, 2:3])
            b4.readers.append(tlg)
            cf_toks = []
            for gi in range(4):
                b5 = banks[5]
                d5 = b5.wdeps()
                tt1 = trop(b5.ap[:, 0:16], lgb[0:16, gi * 128:(gi + 1) * 128],
                           deps=[tlg, t_id] + d5, ident=IDC[0:16, 0:16])
                b5.w = tt1
                te = evac(LT[:, 0:16], b5.ap[:, 0:16], deps=[tt1, lt_tok])
                b5.readers.append(te)
                tm1 = vrmax(MISC[:, 100:101], LT[:, 0:16], deps=[te, lt_tok])
                tms1 = vs(LT[:, 16:32], LT[:, 0:16], MISC[:, 100:101], ALU.is_ge,
                          deps=[tm1])
                tlm = stt(LT[:, 32:48], LT[:, 16:32], -1e30, LT[:, 0:16],
                          ALU.mult, ALU.add, deps=[tms1])
                tm2_ = vrmax(MISC[:, 101:102], LT[:, 32:48], deps=[tlm])
                tms2 = vs(LT[:, 32:48], LT[:, 32:48], MISC[:, 101:102], ALU.is_ge,
                          deps=[tm2_])
                tdd = vt(MISC[:, 102:103], MISC[:, 100:101], MISC[:, 101:102],
                         ALU.subtract, deps=[tm2_, tm1])
                ts1 = act(MISC[:, 103:104], MISC[:, 102:103], AF.Sigmoid, deps=[tdd])
                ts2 = act(MISC[:, 104:105], MISC[:, 102:103], AF.Sigmoid,
                          deps=[tdd], scale=-1.0)
                tc1 = vs(LT[:, 16:32], LT[:, 16:32], MISC[:, 103:104], ALU.mult,
                         deps=[ts1, tms1])
                tc2 = vs(LT[:, 32:48], LT[:, 32:48], MISC[:, 104:105], ALU.mult,
                         deps=[ts2, tms2])
                tcf = vt(LT[:, 16:32], LT[:, 16:32], LT[:, 32:48], ALU.add,
                         deps=[tc1, tc2])
                lt_tok = tcf
                b6 = banks[6]
                d6 = b6.wdeps()
                tt2 = trop(b6.ap[0:16, 0:128], LT[:, 16:32], deps=[tcf, t_id] + d6)
                b6.w = tt2
                tev = evac(coeff[:, gi * 128:(gi + 1) * 128], b6.ap[0:16, 0:128],
                           deps=[tt2, cf_tok])
                b6.readers.append(tev)
                cf_toks.append(tev)
                cf_tok = tev
            wx_toks = []
            for e2 in range(2):
                b7 = banks[7]
                d7 = b7.wdeps()
                for k in range(KT):
                    tmm = mm3op(b7.ap[:, 0:512],
                                ASTH[k][:, e2 * 128:e2 * 128 + 128],
                                ASTL[k][:, e2 * 128:e2 * 128 + 128],
                                xn2h(k), xn2l(k),
                                deps=[stat_last, xn2_tok[k]] + (d7 if k == 0 else []),
                                start=(k == 0), stop=(k == KT - 1))
                b7.w = tmm
                exa = evac(xab[e2], b7.ap[:, 0:512], deps=[tmm, wx_toks[-1] if wx_toks else None])
                b7.readers.append(exa)
                b6 = banks[6]
                d6 = b6.wdeps()
                tcr = mmop(b6.ap[:, 0:512], REPT[:, e2 * 128:(e2 + 1) * 128],
                           coeff, deps=cf_toks + [t_rept] + d6)
                b6.w = tcr
                twx = vt(xab[e2], xab[e2], b6.ap[:, 0:512], ALU.mult,
                         deps=[exa, tcr])
                b6.readers.append(twx)
                thc = cast(WXH[e2], xab[e2], deps=[twx])
                tlc = vt(WXL[e2], xab[e2], WXH[e2], ALU.subtract, deps=[thc])
                wx_toks.append(tlc)
            for mp in range(8):
                bqa, bqb = banks[(2 * mp) % 8], banks[(2 * mp + 1) % 8]
                bd = [bqa.wdeps(), bqb.wdeps()]
                for k in range(KT):
                    lw = ld(wstg, wffnt[k * 128:(k + 1) * 128, mp * 256:(mp + 1) * 256],
                            deps=[wffg[0]])
                    hc = cast(WFFH[k], wstg, deps=[lw, wffg[0]])
                    lc = vt(WFFL[k], wstg, WFFH[k], ALU.subtract, deps=[hc])
                    wffg[0] = lc
                    for mi, b in enumerate((bqa, bqb)):
                        tmm = mm3op(b.ap[:, 0:512],
                                    WFFH[k][:, mi * 128:mi * 128 + 128],
                                    WFFL[k][:, mi * 128:mi * 128 + 128],
                                    xn2h(k), xn2l(k),
                                    deps=[lc, xn2_tok[k]] + (bd[mi] if k == 0 else []),
                                    start=(k == 0), stop=False)
                for kb in range(2):
                    for mi, b in enumerate((bqa, bqb)):
                        mcol = mp * 256 + mi * 128
                        tmm = mm3op(b.ap[:, 0:512],
                                    BSTH[kb][:, mcol:mcol + 128],
                                    BSTL[kb][:, mcol:mcol + 128],
                                    WXH[kb], WXL[kb],
                                    deps=wx_toks + [stat_last], start=False,
                                    stop=(kb == 1))
                        if kb == 1:
                            b.w = tmm
                wffg[0] = tmm
                for mi, b in enumerate((bqa, bqb)):
                    m = mp * 2 + mi
                    ev = act(fstg[mi], b.ap[:, 0:512], AF.Identity,
                             deps=[b.w, fst_tok[mi], t_fb], bias=FB[:, m:m + 1])
                    b.readers.append(ev)
                    tst = st(ffnsp[m * 128:(m + 1) * 128, s0:s0 + 512], fstg[mi],
                             deps=[ev])
                    fst_tok[mi] = tst
            xn2_g[0] = tmm
        if "ffnsp" in dbg_out:
            st(dbg_out["ffnsp"], ffnsp, deps=[fst_tok[0], fst_tok[1]])
        em.barrier()
        ln_residual(ffnsp, x2sp, outt, xoff=0)

    sst = None
    if "p1" in phases:
        phase1()
        em.barrier()
    if "pa" in phases:
        sst = passA()
        em.barrier()
    if "pb" in phases:
        passB(sst)
        em.barrier()
    if "p3" in phases:
        phase3()
        em.barrier()
    if "p4" in phases:
        phase4()
    em.barrier()
    for name in dbg:
        if name in dbg_out and name in dbg_specs and name not in (
                "ccin", "ccout", "attnsp", "x2sp", "ffnsp"):
            st(dbg_out[name], dbg_specs[name][0], deps=em.dma_counts())
    if "vsp2" in dbg_out:
        st(dbg_out["vsp2"], dbg_specs["vsp"][0], deps=em.dma_counts())
    em.barrier()
    em.finalize()
    return nc


def stage_inputs(inputs):
    x = np.asarray(inputs["L0_attn_hidden_states"], np.float32)[0]
    Wq = np.ascontiguousarray(np.asarray(inputs["L0_attn_W_qkvz"], np.float32))
    Wba = np.asarray(inputs["L0_attn_W_ba"], np.float32)
    convw = np.asarray(inputs["L0_attn_conv_weight"], np.float32)
    alog = np.asarray(inputs["L0_attn_A_log"], np.float32)
    dtb = np.asarray(inputs["L0_attn_dt_bias"], np.float32)
    nw = np.asarray(inputs["L0_attn_norm_weight"], np.float32)
    Wout = np.ascontiguousarray(np.asarray(inputs["L0_attn_W_out"], np.float32))
    W = np.asarray(inputs["L0_ffn_W"], np.float32)
    bias = np.asarray(inputs["L0_ffn_bias"], np.float32)
    rw1 = np.asarray(inputs["L0_ffn_rw1"], np.float32)
    rb1 = np.asarray(inputs["L0_ffn_rb1"], np.float32)
    rw2 = np.asarray(inputs["L0_ffn_rw2"], np.float32)
    rb2 = np.asarray(inputs["L0_ffn_rb2"], np.float32)
    A = np.asarray(inputs["L0_ffn_A"], np.float32)
    Bm = np.asarray(inputs["L0_ffn_B"], np.float32)
    gates = np.asarray(inputs["L0_ffn_gates"], np.float32)

    xT = np.ascontiguousarray(x.T)
    perm = np.zeros(64, np.int64)
    for h in range(16):
        for j in range(2):
            perm[2 * h + j] = 4 * h + j
            perm[32 + 2 * h + j] = 4 * h + 2 + j
    wbat = np.ascontiguousarray(Wba[:, perm])
    cwt = np.zeros((64 * 128, 4), np.float32)
    for ci in range(64):
        if ci < 16:
            rows = np.arange(ci * 128, ci * 128 + 128)
        elif ci < 32:
            rows = np.arange(2048 + (ci - 16) * 128, 2048 + (ci - 16) * 128 + 128)
        else:
            rows = np.arange(4096 + (ci - 32) * 128, 4096 + (ci - 32) * 128 + 128)
        cwt[ci * 128:(ci + 1) * 128] = convw[rows]
    smallp = np.zeros((64, 2), np.float32)
    smallp[32:64, 0] = alog
    smallp[32:64, 1] = dtb
    nwc = nw.reshape(128, 1).astype(np.float32)
    wffnt = np.ascontiguousarray(W.T)
    fbias = np.ascontiguousarray(bias.reshape(16, 128).T)
    rw1t = np.ascontiguousarray(rw1.T)
    rw2t = np.ascontiguousarray(rw2[:16].T)
    astt = np.ascontiguousarray(A.reshape(256, D).T)
    bstt = np.ascontiguousarray(Bm.transpose(0, 2, 1).reshape(256, D))
    rept = np.zeros((16, 256), np.float32)
    for e in range(16):
        rept[e, e * 16:(e + 1) * 16] = 1.0
    ident = np.eye(128, dtype=np.float32)
    ii = np.arange(128)
    ml = np.where(ii[None, :] < ii[:, None], 0.0, -1e30).astype(np.float32)
    mu = np.where(ii[None, :] > ii[:, None], 0.0, -1e30).astype(np.float32)
    mui = np.where(ii[None, :] >= ii[:, None], 0.0, -1e30).astype(np.float32)
    masks = np.concatenate([ml, mu, mui], axis=1)

    common = dict(
        wqkvz=Wq, wbat=wbat, convwt=cwt, smallp=smallp, nwcd=nwc,
        woutt=Wout, wffnt=wffnt, fbiast=fbias, rw1td=rw1t,
        rb1cd=rb1.reshape(64, 1), rw2td=rw2t, rb2cd=rb2[:16].reshape(16, 1),
        gatesd=gates.reshape(16, 1), asttd=astt, bsttd=bstt, reptd=rept,
        identd=ident, masksd=masks,
    )
    in_maps = []
    for ci in range(NC_):
        s0 = ci * T
        xtc = np.zeros((D, 3 + T), np.float32)
        lo = max(0, s0 - 3)
        nh = s0 - lo
        if nh:
            xtc[:, 3 - nh:3] = xT[:, lo:s0]
        xtc[:, 3:] = xT[:, s0:s0 + T]
        cm = np.zeros((128, 8), np.float32)
        cm[:, :ci] = 1.0
        m = dict(common)
        m["xt"] = xtc
        m["cmaskd"] = cm
        in_maps.append(m)
    return in_maps


def kernel(**inputs):
    in_maps = stage_inputs(inputs)
    nc = build()
    res = run_bass_kernel_spmd(nc, in_maps, core_ids=list(range(NC_)))
    outs = []
    for ci in range(NC_):
        outs.append(np.asarray(res.results[ci]["outt"]).T)
    y = np.concatenate(outs, axis=0)[None, :, :].astype(np.float32)
    return y
